# version 17
# baseline (speedup 1.0000x reference)
import math
import numpy as np
import ml_dtypes
import concourse.bass as bass
import concourse.mybir as mybir
from concourse.bass_utils import run_bass_kernel_spmd
from concourse.aot_env import allow_ici_all_to_all
from contextlib import ExitStack

F32 = mybir.dt.float32
BF16 = mybir.dt.bfloat16
F16 = mybir.dt.float16
I32 = mybir.dt.int32
AF = mybir.ActivationFunctionType
ALU = mybir.AluOpType
AX = mybir.AxisListType.X

S = 8192
D = 4096
HD = 128
NC = 8
TPC = S // NC
EPS = 1e-5
SCL = 1.0 / math.sqrt(HD)
LNB = -0.5 * math.log(HD)

ENGS = ("sync", "vector", "scalar", "tensor", "gpsimd")
DBG = []


class Em:

    def __init__(self, nc, sems, cnt, marks, active, engines):
        self.nc = nc
        self.sems = sems
        self.cnt = cnt
        self.M = marks
        self.active = active
        self.e = engines
        self.record = active is None

    def _mark(self, key, ctr):
        if key is None:
            return
        if self.record:
            assert key not in self.M, f"duplicate mark {key}"
            self.M[key] = self.cnt[ctr]
        else:
            assert self.M[key] == self.cnt[ctr], (
                f"mark mismatch {key}: {self.M[key]} != {self.cnt[ctr]} "
                f"(engine pass {self.active})")

    def op(self, eng, build, ctr=None, inc=1, mark=None):
        if ctr is not None:
            self.cnt[ctr] += inc
        if self.active == eng:
            bi = build(self.e[eng])
            if ctr is not None:
                bi.then_inc(self.sems[ctr], inc)
        if mark is not None:
            assert ctr is not None
            self._mark(mark, ctr)

    def dma(self, eng, out, in_, ctr="cL", mark=None):
        self.op(eng, lambda e: e.dma_start(out=out, in_=in_), ctr, 16, mark)

    def wait(self, eng, ctr, val):
        if self.active == eng:
            v = val() if callable(val) else val
            self.e[eng].wait_ge(self.sems[ctr], v)

    def waitm(self, eng, ctr, key):
        self.wait(eng, ctr, lambda: self.M[key])

    def dver(self, ctr, lkey, vkey):
        self.wait("sync", ctr, lambda: self.M[lkey])
        self.op("sync", lambda s: s.nop(), "cSy", 1, vkey)

    def fop(self, eng, build, ctr, mark=None):
        self.op(eng, build, ctr, 1, mark)
        self.wait(eng, ctr, self.cnt[ctr])


def build_nc():
    nc = bass.Bass("TRN2", target_bir_lowering=False, num_devices=NC)

    x_c = nc.dram_tensor("x_c", [TPC, D], F32, kind="ExternalInput")
    wqt_s = nc.dram_tensor("wqt_s", [D, 512], F32, kind="ExternalInput")
    wkt_s = nc.dram_tensor("wkt_s", [D, 128], F32, kind="ExternalInput")
    wvt_s = nc.dram_tensor("wvt_s", [D, 128], F32, kind="ExternalInput")
    wot_s = nc.dram_tensor("wot_s", [512, D], F32, kind="ExternalInput")
    pkt_s = nc.dram_tensor("pkt_s", [D // 2, 512], I32, kind="ExternalInput")
    scx_s = nc.dram_tensor("scx_s", [D // 2, 512], F16, kind="ExternalInput")
    cost = nc.dram_tensor("cost", [HD, S], F32, kind="ExternalInput")
    sint = nc.dram_tensor("sint", [HD, S], F32, kind="ExternalInput")
    out_c = nc.dram_tensor("out_c", [TPC, D], F32, kind="ExternalOutput")
    dbg_t = {}
    if DBG:
        for nm, shp, dt in [
                ("dbg_k", [128, S], BF16), ("dbg_v", [128, S], BF16),
                ("dbg_qr", [128, 4 * 4096], BF16),
                ("dbg_bias", [128, 4], F32),
                ("dbg_xnt", [D, TPC], BF16),
                ("dbg_atslab", [4096, TPC], BF16),
                ("dbg_atloc", [4096, TPC], BF16),
                ("dbg_wf", [512, D], BF16), ("dbg_wo", [512, D], BF16),
                ("dbg_o", [128, 4 * D], BF16), ("dbg_h", [128, 4 * D], F32),
                ("dbg_y", [128, 4 * D], BF16),
                ("dbg_slab", [512, D], BF16),
                ("dbg_wfst", [128, 32 * 512], BF16)]:
            dbg_t[nm] = nc.dram_tensor(nm, shp, dt, kind="ExternalOutput")

    xsla = nc.dram_tensor("xsla", [D, 512], BF16, kind="Internal")
    xslb = nc.dram_tensor("xslb", [D, 512], BF16, kind="Internal")
    xga = nc.dram_tensor("xga", [NC * D, 512], BF16, kind="Internal",
                         addr_space="Shared")
    xgb = nc.dram_tensor("xgb", [NC * D, 512], BF16, kind="Internal",
                         addr_space="Shared")
    at_slab = nc.dram_tensor("at_slab", [4096, TPC], BF16, kind="Internal")
    at_loc = nc.dram_tensor("at_loc", [4096, TPC], BF16, kind="Internal")
    comb_slab = nc.dram_tensor("comb_slab", [1024, D], BF16, kind="Internal")
    comb_g = nc.dram_tensor("comb_g", [NC * 1024, D], BF16, kind="Internal",
                            addr_space="Shared")
    wo_slab = comb_slab[0:512, :]
    wf_slab = comb_slab[512:1024, :].rearrange("(p four) n -> p (four n)",
                                               four=4)

    ident_d = nc.inline_tensor(np.eye(128, dtype=ml_dtypes.bfloat16),
                               name="identc")
    R = np.zeros((128, 128), np.float32)
    for i in range(64):
        R[i, 64 + i] = -1.0
        R[64 + i, i] = 1.0
    rT_d = nc.inline_tensor(R.T.copy().astype(ml_dtypes.bfloat16), name="rTc")
    onesc_d = nc.inline_tensor(np.ones((128, 1), ml_dtypes.bfloat16),
                               name="onescc")
    onesr_d = nc.inline_tensor(np.ones((1, 128), np.float32), name="onesrc")
    msk = np.zeros((128, 4 * 512), np.float32)
    ki = np.arange(128)[:, None]
    qi = np.arange(512)[None, :]
    for j in range(4):
        msk[:, j * 512:(j + 1) * 512] = (qi >= ki + 128 * j)
    masks_d = nc.inline_tensor(msk.astype(ml_dtypes.bfloat16), name="masksc")

    engines = dict(sync=nc.sync, vector=nc.vector, scalar=nc.scalar,
                   tensor=nc.tensor, gpsimd=nc.gpsimd)

    CTRS = ["cL", "cLa", "cLb", "cLc", "cLd", "cLe", "cLf", "cLX", "cSy",
            "cP1", "cSt", "cStW", "cCC",
            "cCCo", "cCCf", "cCCa", "cT1", "cT2", "cT3", "cT4",
            "cA1", "cA2", "cV1", "cV2", "cV3"]
    sems = {n: nc.alloc_semaphore(n) for n in CTRS}
    G = {n: 0 for n in CTRS}
    M = {}

    def run_phase(name, fn):
        nonlocal G
        dry = Em(nc, sems, dict(G), M, None, engines)
        fn(dry)
        with nc.Block(name) as block:
            for engname in ENGS:
                def body(e, _en=engname):
                    fn(Em(nc, sems, dict(G), M, _en, engines))
                getattr(block, engname if engname != "sync" else "sync")(body)
        G = dry.cnt

    stack = ExitStack()
    sb = lambda name, shape, dt: stack.enter_context(
        nc.sbuf_tensor(name, shape, dt))

    ident_sb = sb("ident_sb", [128, 128], BF16)
    rT_sb = sb("rT_sb", [128, 128], BF16)
    onesc_sb = sb("onesc_sb", [128, 1], BF16)
    onesr_sb = sb("onesr_sb", [1, 128], F32)
    masks_sb = sb("masks_sb", [128, 2048], BF16)
    biasq_sb = sb("biasq_sb", [128, 4], F32)
    rmq_sb = sb("rmq_sb", [1, 4], F32)
    rmk_sb = sb("rmk_sb", [1, 1], F32)
    tiny_sb = sb("tiny_sb", [1, 16], F32)
    sm_sb = sb("sm_sb", [128, 1], F32)
    ss_sb = sb("ss_sb", [128, 1], F32)
    m_sb = sb("m_sb", [128, 1], F32)
    va_sb = sb("va_sb", [128, 1], F32)
    rs_sb = sb("rs_sb", [128, 1], F32)
    scr_bf = sb("scr_bf", [128, D], BF16)
    rec_sb = sb("rec_sb", [1, 512], F32)
    recb_sb = sb("recb_sb", [128, 512], F32)
    pt_sb = sb("pt_sb", [128, 2048], BF16)
    atst_sb = sb("atst_sb", [128, 1024], BF16)
    ceps_sb = sb("ceps_sb", [128, 1], F32)
    clnb_sb = sb("clnb_sb", [128, 1], F32)

    def ph_consts(em):
        em.dma("sync", ident_sb[:, :], ident_d[:, :])
        em.dma("sync", rT_sb[:, :], rT_d[:, :])
        em.dma("sync", onesc_sb[:, :], onesc_d[:, :])
        em.dma("sync", onesr_sb[:, :], onesr_d[:, :])
        em.dma("sync", masks_sb[:, :], masks_d[:, :], mark="constsL")
        em.waitm("sync", "cL", "constsL")
        em.op("vector", lambda v: v.memset(rmq_sb[:, :], 0.0))
        em.op("vector", lambda v: v.memset(ceps_sb[:, :], EPS))
        em.op("vector", lambda v: v.memset(clnb_sb[:, :], LNB))
        em.op("vector", lambda v: v.memset(rmk_sb[:, :], 0.0), "cV1", 1,
              "rm_init")

    run_phase("consts", ph_consts)
    nc.all_engine_barrier()

    p1s = ExitStack()
    sb1 = lambda n, sh, dt: p1s.enter_context(nc.sbuf_tensor(n, sh, dt))
    xin_sb = sb1("xin_sb", [128, 2 * D], F32)
    xnb_sb = sb1("xnb_sb", [128, 2 * D], BF16)
    xnT_st = sb1("xnT_st", [128, 32 * 1024], BF16)
    tp_ps = [p1s.enter_context(nc.psum_tensor(f"tp_ps{i}", [128, 128], BF16))
             for i in range(2)]
    dm1_ps = p1s.enter_context(nc.psum_tensor("dm1_ps", [128, 128], BF16))

    def ln_stats(em, src_ap, pfx):
        em.op("scalar", lambda s: s.activation(
            out=scr_bf[:, 0:src_ap.shape[1]], in_=src_ap, func=AF.Copy,
            accum_out=sm_sb[:, :]))
        em.op("scalar", lambda s: s.activation(
            out=scr_bf[:, 0:src_ap.shape[1]], in_=src_ap, func=AF.Square,
            accum_out=ss_sb[:, :]), "cA1", 1, pfx + "st")
        em.waitm("vector", "cA1", pfx + "st")
        n = src_ap.shape[1]
        em.fop("vector", lambda v: v.tensor_scalar_mul(m_sb[:, :], sm_sb[:, :],
                                                       1.0 / n), "cV1")
        em.fop("vector", lambda v: v.tensor_scalar_mul(va_sb[:, :],
                                                       ss_sb[:, :], 1.0 / n),
               "cV1")
        em.fop("vector", lambda v: v.tensor_mul(ss_sb[:, :], m_sb[:, :],
                                                m_sb[:, :]), "cV1")
        em.fop("vector", lambda v: v.tensor_sub(va_sb[:, :], va_sb[:, :],
                                                ss_sb[:, :]), "cV1",
               pfx + "mv")
        em.waitm("scalar", "cV1", pfx + "mv")
        em.fop("scalar", lambda s: s.activation(out=va_sb[:, :],
                                                in_=va_sb[:, :], func=AF.Ln,
                                                bias=ceps_sb[:, 0:1],
                                                scale=1.0), "cA2")
        em.fop("scalar", lambda s: s.activation(out=rs_sb[:, :],
                                                in_=va_sb[:, :],
                                                func=AF.Exp, scale=-0.5),
               "cA2", pfx + "rs")

    def ph1(em):
        for tt in range(8):
            xt = xin_sb[:, (tt % 2) * D:(tt % 2 + 1) * D]
            xo = xnb_sb[:, (tt % 2) * D:(tt % 2 + 1) * D]
            if tt >= 2:
                em.waitm("sync", "cV2", f"p1ap{tt - 2}")
            em.dma("sync", xt, x_c[tt * 128:(tt + 1) * 128, :],
                   ctr="cLa" if tt % 2 == 0 else "cLb", mark=f"p1x{tt}")
            em.waitm("scalar", "cLa" if tt % 2 == 0 else "cLb", f"p1x{tt}")
            if tt >= 1:
                em.waitm("scalar", "cV2", f"p1ap{tt - 1}")
            ln_stats(em, xt, f"p1s{tt}")
            em.waitm("vector", "cA2", f"p1s{tt}rs")
            if tt >= 2:
                em.waitm("vector", "cT1", f"p1tp{tt - 2}_31")
            em.op("vector", lambda v, xo=xo, xt=xt: v.tensor_scalar(
                xo, xt, m_sb[:, :], rs_sb[:, :], ALU.subtract, ALU.mult),
                "cV2", 1, f"p1ap{tt}")
            for dc in range(32):
                g = 32 * tt + dc
                em.waitm("tensor", "cV2", f"p1ap{tt}")
                if g >= 2:
                    em.waitm("tensor", "cV3", f"p1ev{g - 2}")
                em.op("tensor", lambda t, xo=xo, dc=dc, g=g: t.transpose(
                    tp_ps[g % 2][:, :], xo[:, dc * 128:(dc + 1) * 128],
                    ident_sb[:, :]), "cT1", 1, f"p1tp{tt}_{dc}")
                if dc == 31:
                    em.op("tensor", lambda t: t.transpose(
                        dm1_ps[:, :], ident_sb[:, :], ident_sb[:, :]),
                        "cT1", 1)
                em.wait("vector", "cT1",
                        (lambda k=f"p1tp{tt}_{dc}": em.M[k] + 1))
                em.op("vector", lambda v, dc=dc, tt=tt, g=g: v.tensor_copy(
                    xnT_st[:, dc * 1024 + tt * 128:dc * 1024 + tt * 128 + 128],
                    tp_ps[g % 2][:, :]), "cV3", 1, f"p1ev{g}")
        em.waitm("gpsimd", "cV3", "p1ev127")
        for dc in range(32):
            em.dma("gpsimd", xsla[dc * 128:(dc + 1) * 128, :],
                   xnT_st[:, dc * 1024:dc * 1024 + 512], ctr="cSt",
                   mark=f"p1sa{dc}")
        em.waitm("gpsimd", "cSt", "p1sa31")
        em.op("gpsimd", lambda g: g.collective_compute(
            "AllGather", ALU.bypass, replica_groups=[list(range(NC))],
            ins=[xsla[:, :]], outs=[xga[:, :]]), "cCC", 1, "agxn1")
        em.waitm("gpsimd", "cV3", "p1ev255")
        for dc in range(32):
            em.dma("gpsimd", xslb[dc * 128:(dc + 1) * 128, :],
                   xnT_st[:, dc * 1024 + 512:(dc + 1) * 1024], ctr="cSt",
                   mark=f"p1sb{dc}")
        em.waitm("gpsimd", "cSt", "p1sb31")
        em.waitm("gpsimd", "cCC", "agxn1")
        em.op("gpsimd", lambda g: g.collective_compute(
            "AllGather", ALU.bypass, replica_groups=[list(range(NC))],
            ins=[xslb[:, :]], outs=[xgb[:, :]]), "cCC", 1, "agxn2")



    p2s = ExitStack()
    sb2 = lambda n, sh, dt: p2s.enter_context(nc.sbuf_tensor(n, sh, dt))
    ld32_sb = sb2("ld32_sb", [128, 2 * D], F32)
    cvt_sb = sb2("cvt_sb", [128, 2 * D], BF16)
    pki_sb = sb2("pki_sb", [128, 2 * 512], I32)
    scf_sb = sb2("scf_sb", [128, 2 * 512], F16)
    loi_sb = sb2("loi_sb", [128, 512], I32)
    hii_sb = sb2("hii_sb", [128, 512], I32)
    lof_sb = sb2("lof_sb", [128, 512], F32)
    hif_sb = sb2("hif_sb", [128, 512], F32)
    sclf_sb = sb2("sclf_sb", [128, 512], F32)
    wfT_st = xnT_st[:, 0:16384]

    def ph2(em):
        for ch in range(4):
            lt = ld32_sb[:, (ch % 2) * D:(ch % 2 + 1) * D]
            ct = cvt_sb[:, (ch % 2) * D:(ch % 2 + 1) * D]
            if ch >= 2:
                em.waitm("sync", "cV1", f"p2wo{ch - 2}")
            em.dma("sync", lt, wot_s[ch * 128:(ch + 1) * 128, :],
                   ctr="cLc" if ch % 2 == 0 else "cLd", mark=f"p2wl{ch}")
            em.waitm("vector", "cLc" if ch % 2 == 0 else "cLd", f"p2wl{ch}")
            if ch >= 2:
                em.wait("vector", "cStW", 16 * ch)
            em.op("vector", lambda v, ct=ct, lt=lt: v.tensor_copy(ct, lt),
                  "cV1", 1, f"p2wo{ch}")
            em.waitm("gpsimd", "cV1", f"p2wo{ch}")
            em.dma("gpsimd", wo_slab[ch * 128:(ch + 1) * 128, :], ct,
                   ctr="cStW", mark=f"p2ws{ch}")
        for aa in range(16):
            pk = pki_sb[:, (aa % 2) * 512:(aa % 2 + 1) * 512]
            sf = scf_sb[:, (aa % 2) * 512:(aa % 2 + 1) * 512]
            if aa >= 2:
                em.waitm("sync", "cV2", f"p2dq{aa - 2}")
            sem = "cLe" if aa % 2 == 0 else "cLf"
            em.dma("sync", pk, pkt_s[aa * 128:(aa + 1) * 128, :], ctr=sem)
            em.dma("sync", sf, scx_s[aa * 128:(aa + 1) * 128, :], ctr=sem,
                   mark=f"p2pl{aa}")
            em.waitm("vector", sem, f"p2pl{aa}")
            if aa == 0:
                em.waitm("vector", "cSt", "p1sb31")
            em.op("vector", lambda v, pk=pk: v.tensor_scalar(
                loi_sb[:, :], pk, 15, None, ALU.bitwise_and))
            em.op("vector", lambda v, pk=pk: v.tensor_scalar(
                hii_sb[:, :], pk, 4, 15, ALU.logical_shift_right,
                ALU.bitwise_and))
            em.op("vector", lambda v: v.tensor_copy(lof_sb[:, :],
                                                    loi_sb[:, :]))
            em.op("vector", lambda v: v.tensor_copy(hif_sb[:, :],
                                                    hii_sb[:, :]))
            em.op("vector", lambda v, sf=sf: v.tensor_copy(sclf_sb[:, :], sf))
            em.op("vector", lambda v, aa=aa: v.scalar_tensor_tensor(
                wfT_st[:, (2 * aa) * 512:(2 * aa) * 512 + 512],
                lof_sb[:, :], -8.0, sclf_sb[:, :], ALU.add, ALU.mult))
            em.op("vector", lambda v, aa=aa: v.scalar_tensor_tensor(
                wfT_st[:, (2 * aa + 1) * 512:(2 * aa + 1) * 512 + 512],
                hif_sb[:, :], -8.0, sclf_sb[:, :], ALU.add, ALU.mult),
                "cV2", 1, f"p2dq{aa}")
        em.waitm("gpsimd", "cV2", "p2dq15")
        for kc in range(32):
            em.dma("gpsimd", wf_slab[:, kc * 512:(kc + 1) * 512],
                   wfT_st[:, kc * 512:(kc + 1) * 512], ctr="cSt",
                   mark=f"p2fs{kc}")
        if DBG:
            em.waitm("sync", "cV2", "p2dq15")
            em.dma("sync", dbg_t["dbg_wfst"][:, :], wfT_st[:, :], ctr="cSt")

    run_phase("p1", lambda em: (ph1(em), ph2(em)))
    nc.all_engine_barrier()
    p2s.close()
    p1s.close()

    p34s = ExitStack()
    sb3 = lambda n, sh, dt: p34s.enter_context(nc.sbuf_tensor(n, sh, dt))
    wq_sb = sb3("wq_sb", [128, 32 * 512], BF16)
    wk_sb = sb3("wk_sb", [128, 32 * 128], BF16)
    wv_sb = sb3("wv_sb", [128, 32 * 128], BF16)
    k_sb = sb3("k_sb", [128, S], BF16)
    v_sb = sb3("v_sb", [128, S], BF16)
    qr_sb = sb3("qr_sb", [128, 4 * 4096], BF16)
    cosb_sb = sb3("cosb_sb", [128, 4096], BF16)
    sinb_sb = sb3("sinb_sb", [128, 4096], BF16)
    xnT_sb = sb3("xnT_sb", [128, 32 * 512], BF16)
    wld_sb = sb3("wld_sb", [128, 4096], F32)
    acc_sb = wld_sb[:, 0:512]
    accp_sb = wld_sb[:, 512:1024]
    qraw_sb = sb3("qraw_sb", [128, 2 * 512], BF16)
    t1_sb = pt_sb[:, 1024:1536]
    t2_sb = pt_sb[:, 1536:2048]
    sq_sb = sb3("sq_sb", [128, 512], BF16)
    accb_sb = sq_sb
    vtmp_sb = sb3("vtmp_sb", [128, 2 * 512], BF16)
    q_ps = [p34s.enter_context(nc.psum_tensor(f"q_ps{i}", [128, 512], F32))
            for i in range(2)]
    sx_ps = [p34s.enter_context(nc.psum_tensor(f"sx_ps{i}", [128, 512], F32))
             for i in range(2)]
    rot_ps = p34s.enter_context(nc.psum_tensor("rot_ps", [128, 512], F32))
    n_ps = p34s.enter_context(nc.psum_tensor("n_ps", [1, 512], F32))
    o_ps = p34s.enter_context(nc.psum_tensor("o_ps", [128, 512], F32))
    tp3_ps0 = p34s.enter_context(nc.psum_tensor("tp3_ps0", [128, 128],
                                                 BF16))
    tp3_ps = [tp3_ps0, tp3_ps0]
    s_ps = q_ps + sx_ps
    sum_ps = n_ps
    b_ps = rot_ps

    def ph3(em, hf):
        if hf == 0:
            em.waitm("gpsimd", "cSt", "p2fs31")
            em.waitm("gpsimd", "cStW", "p2ws3")
            em.waitm("gpsimd", "cCC", "agxn2")
            em.op("gpsimd", lambda g: g.collective_compute(
                "AllGather", ALU.bypass, replica_groups=[list(range(NC))],
                ins=[comb_slab[:, :]], outs=[comb_g[:, :]]), "cCCo", 1,
                "agwo")
        if hf == 0:
            em.waitm("sync", "cSt", "p2fs31")
            em.waitm("sync", "cStW", "p2ws3")
            for sd in range(4):
                if sd >= 1:
                    em.waitm("sync", "cV2", f"wqcv{sd - 1}")
                em.dma("sync",
                       wld_sb[:, :].rearrange("p (c n) -> p c n", n=512),
                       wqt_s[sd * 1024:(sd + 1) * 1024, :].rearrange(
                           "(c p) n -> p c n", p=128), mark=f"wql{sd}")
                em.dver("cL", f"wql{sd}", f"wqv{sd}")
                em.waitm("vector", "cSy", f"wqv{sd}")
                em.op("vector", lambda v, sd=sd: v.tensor_copy(
                    wq_sb[:, sd * 4096:(sd + 1) * 4096], wld_sb[:, :]),
                    "cV2", 1, f"wqcv{sd}")
        pf = f"h{hf}"
        em.waitm("sync", "cV2", "wqcv3" if hf == 0 else "h0sincv")
        em.dma("sync", wld_sb[:, :], cost[:, hf * 4096:(hf + 1) * 4096],
               mark=pf + "cosl")
        em.dver("cL", pf + "cosl", pf + "cosv")
        em.waitm("vector", "cSy", pf + "cosv")
        em.op("vector", lambda v: v.tensor_copy(cosb_sb[:, :], wld_sb[:, :]),
              "cV2", 1, pf + "coscv")
        em.waitm("sync", "cV2", pf + "coscv")
        em.dma("sync", wld_sb[:, :], sint[:, hf * 4096:(hf + 1) * 4096],
               mark=pf + "sinl")
        em.dver("cL", pf + "sinl", pf + "sinv")
        em.waitm("vector", "cSy", pf + "sinv")
        em.op("vector", lambda v: v.tensor_copy(sinb_sb[:, :], wld_sb[:, :]),
              "cV2", 1, pf + "sincv")
        if hf == 0:
            for nm, tgt, dst in (("wk", wkt_s, wk_sb), ("wv", wvt_s, wv_sb)):
                em.waitm("sync", "cV2",
                         pf + "sincv" if nm == "wk" else "wkcv")
                em.dma("sync",
                       wld_sb[:, :].rearrange("p (c n) -> p c n", n=128),
                       tgt.rearrange("(c p) n -> p c n", p=128),
                       mark=nm + "l")
                em.dver("cL", nm + "l", nm + "v")
                em.waitm("vector", "cSy", nm + "v")
                em.op("vector", lambda v, dst=dst: v.tensor_copy(
                    dst[:, :], wld_sb[:, :]), "cV2", 1, nm + "cv")
        for idx, tl in enumerate(ORD0):
            tm = hf * 8 + tl
            if idx >= 1:
                ptm = hf * 8 + ORD0[idx - 1]
                em.waitm("sync", "cT1", f"u{ptm * 6 + 5}mm")
            elif hf == 1:
                em.waitm("sync", "cT1", f"u{(ORD0[-1]) * 6 + 5}mm")
            sl, c0 = tm // 2, (tm % 2) * 512
            em.wait("sync", "cCC",
                    lambda c0=c0: em.M["agxn1" if c0 == 0 else "agxn2"])
            xsrc = xga if c0 == 0 else xgb
            em.dma("sync",
                   xnT_sb[:, :].rearrange("p (c n) -> p c n", n=512),
                   xsrc[sl * D:(sl + 1) * D, :].rearrange(
                       "(c p) n -> p c n", p=128), mark=f"xnl{tm}")
            em.dver("cL", f"xnl{tm}", f"xnv{tm}")
            for uu in range(6):
                u = tm * 6 + uu
                qrw = qraw_sb[:, (u % 2) * 512:(u % 2) * 512 + 512] \
                    if uu < 5 else vtmp_sb[:, (tm % 2) * 512:(tm % 2) * 512
                                           + 512]
                em.waitm("tensor", "cSy", f"xnv{tm}")
                if pprev.get(u) is not None:
                    em.waitm("tensor", "cA1", f"u{pprev[u]}ev")
                for dc in range(32):
                    if uu < 4:
                        lhs = wq_sb[:, dc * 512 + uu * 128:
                                    dc * 512 + uu * 128 + 128]
                    elif uu == 4:
                        lhs = wk_sb[:, dc * 128:dc * 128 + 128]
                    else:
                        lhs = wv_sb[:, dc * 128:dc * 128 + 128]
                    em.op("tensor", lambda t, lhs=lhs, dc=dc, u=u: t.matmul(
                        q_ps[u % 2][:, :], lhs,
                        xnT_sb[:, dc * 512:dc * 512 + 512],
                        start=(dc == 0), stop=(dc == 31)),
                        "cT1" if dc == 31 else None, 1,
                        f"u{u}mm" if dc == 31 else None)
                em.waitm("scalar", "cT1", f"u{u}mm")
                if uu < 5 and qraw_prev.get(u) is not None:
                    em.waitm("scalar", "cV1", f"u{qraw_prev[u]}rp")
                if uu == 5 and tm >= 2:
                    em.waitm("scalar", "cT4", f"u{(tm - 2) * 6 + 5}tp3")
                em.op("scalar", lambda s, qrw=qrw, u=u: s.copy(
                    qrw, q_ps[u % 2][:, :]), "cA1", 1, f"u{u}ev")
                if uu < 5:
                    em.waitm("tensor", "cA1", f"u{u}ev")
                    qk_prev = prev_qk.get(u)
                    if qk_prev is not None:
                        em.waitm("tensor", "cV1", f"u{qk_prev}rp")
                    em.op("tensor", lambda t, qrw=qrw: t.matmul(
                        rot_ps[:, :], rT_sb[:, :], qrw, start=True,
                        stop=True), "cT2", 1, f"u{u}rot")
                    em.waitm("vector", "cT2", f"u{u}rot")
                    if uu < 4:
                        dst = qr_sb[:, uu * 4096 + tl * 512:
                                    uu * 4096 + tl * 512 + 512]
                    else:
                        dst = k_sb[:, tm * 512:tm * 512 + 512]
                    co = cosb_sb[:, tl * 512:tl * 512 + 512]
                    si = sinb_sb[:, tl * 512:tl * 512 + 512]
                    em.op("vector", lambda v, qrw=qrw, co=co: v.tensor_mul(
                        t1_sb[:, :], qrw, co))
                    em.op("vector", lambda v, si=si: v.tensor_mul(
                        t2_sb[:, :], rot_ps[:, :], si))
                    em.op("vector", lambda v, dst=dst: v.tensor_add(
                        dst, t1_sb[:, :], t2_sb[:, :]), "cV1", 1, f"u{u}rp")
                    em.waitm("scalar", "cV1", f"u{u}rp")
                    if qk_prev is not None:
                        em.waitm("scalar", "cT3", f"u{qk_prev}nm")
                    em.op("scalar", lambda s, dst=dst: s.activation(
                        out=sq_sb[:, :], in_=dst, func=AF.Square),
                        "cA2", 1, f"u{u}sq")
                    em.waitm("tensor", "cA2", f"u{u}sq")
                    if qk_prev is not None:
                        em.waitm("tensor", "cV2", f"u{qk_prev}nr")
                    em.op("tensor", lambda t: t.matmul(
                        n_ps[0:1, :], onesc_sb[:, :], sq_sb[:, :],
                        start=True, stop=True), "cT3", 1, f"u{u}nm")
                    em.waitm("vector", "cT3", f"u{u}nm")
                    em.fop("vector", lambda v: v.reduce_max(
                        tiny_sb[:, 0:1], n_ps[0:1, :], axis=AX), "cV2")
                    tgt = (rmq_sb[0:1, uu:uu + 1] if uu < 4
                           else rmk_sb[0:1, 0:1])
                    em.fop("vector", lambda v, tgt=tgt: v.tensor_max(
                        tgt, tgt, tiny_sb[:, 0:1]), "cV2", f"u{u}nr")
                else:
                    em.waitm("tensor", "cA1", f"u{u}ev")
                    for j in range(4):
                        g = 4 * tm + j
                        if vprev.get(g) is not None:
                            em.waitm("tensor", "cV3", f"vev{vprev[g]}")
                        em.op("tensor", lambda t, qrw=qrw, j=j, g=g:
                              t.transpose(tp3_ps[g % 2][:, :],
                                          qrw[:, j * 128:j * 128 + 128],
                                          ident_sb[:, :]),
                              "cT4", 1,
                              f"u{u}tp3" if j == 3 else f"u{u}tp{j}")
                        em.wait("vector", "cT4",
                                (lambda k=(f"u{u}tp3" if j == 3
                                           else f"u{u}tp{j}"): em.M[k]))
                        em.op("vector", lambda v, tm=tm, j=j, g=g:
                              v.tensor_copy(
                                  v_sb[:, (tm * 4 + j) * 128:
                                       (tm * 4 + j) * 128 + 128],
                                  tp3_ps[g % 2][:, :]), "cV3", 1, f"vev{g}")

    def ph3bias(em, hf):
        em.waitm("vector", "cV2", f"u{(hf * 8 + 7) * 6 + 4}nr")
        for h in range(4):
            if h >= 1:
                em.waitm("vector", "cA2", f"b{hf}_{h - 1}B")
            em.fop("vector", lambda v, h=h: v.tensor_mul(
                tiny_sb[:, 1:2], rmq_sb[0:1, h:h + 1], rmk_sb[0:1, 0:1]),
                "cV1", f"b{hf}_{h}mul")
            em.waitm("scalar", "cV1", f"b{hf}_{h}mul")
            em.fop("scalar", lambda s: s.activation(
                out=tiny_sb[:, 3:4], in_=tiny_sb[:, 1:2], func=AF.Ln), "cA2")
            em.fop("scalar", lambda s: s.activation(
                out=tiny_sb[:, 3:4], in_=tiny_sb[:, 3:4], func=AF.Exp,
                scale=0.5, bias=clnb_sb[0:1, 0:1]), "cA2", f"b{hf}_{h}B")
            em.waitm("vector", "cA2", f"b{hf}_{h}B")
            if h >= 1:
                em.waitm("vector", "cT2", f"b{hf}_{h - 1}mm")
            em.fop("vector", lambda v: v.tensor_scalar_mul(
                tiny_sb[:, 2:3], tiny_sb[:, 3:4], -1.0), "cV2",
                f"b{hf}_{h}ng")
            em.waitm("tensor", "cV2", f"b{hf}_{h}ng")
            if h == 0:
                em.waitm("tensor", "cV1", f"u{(hf * 8 + 7) * 6 + 4}rp")
            if h >= 1:
                em.waitm("tensor", "cA1", f"b{hf}_{h - 1}cp")
            em.op("tensor", lambda t: t.matmul(
                rot_ps[:, 0:1], onesr_sb[:, :], tiny_sb[0:1, 2:3],
                start=True, stop=True), "cT2", 1, f"b{hf}_{h}mm")
            em.waitm("scalar", "cT2", f"b{hf}_{h}mm")
            em.op("scalar", lambda s, h=h: s.copy(
                biasq_sb[:, h:h + 1], rot_ps[:, 0:1]), "cA1", 1,
                f"b{hf}_{h}cp")
        em.wait("vector", "cA2", lambda: em.M[f"b{hf}_3B"])
        em.op("vector", lambda v: v.memset(rmq_sb[:, :], 0.0))
        em.op("vector", lambda v: v.memset(rmk_sb[:, :], 0.0), "cV3", 1,
              f"b{hf}rst")

    def ph4(em, hf):
        pairs = [(hf * 8 + tl, h) for tl in range(8) for h in range(4)]
        for (tm, h) in pairs:
            p = pair_idx[(tm, h)]
            nk = 4 * tm + 4
            tl = tm % 8
            qrh = qr_sb[:, h * 4096 + tl * 512:h * 4096 + tl * 512 + 512]

            def mm_s(kc):
                g = chunk_g[(tm, h, kc)]
                if g >= 4:
                    em.waitm("tensor", "cA1", f"e{g - 4}")
                em.op("tensor", lambda t, kc=kc, g=g, qrh=qrh: t.matmul(
                    s_ps[g % 4][:, :], k_sb[:, kc * 128:kc * 128 + 128],
                    qrh, start=True, stop=True, skip_group_check=True),
                    "cT1", 1, f"s{g}")

            if p >= 1:
                em.waitm("tensor", "cV3", f"ae{p - 1}")
                em.waitm("tensor", "cV2", f"rc{p - 1}")
            for kk in range(min(4, nk)):
                mm_s(kk)
            for kc in range(nk):
                g = chunk_g[(tm, h, kc)]
                if kc >= 4 * tm:
                    em.waitm("tensor", "cV1", f"mk{g}")
                else:
                    em.waitm("tensor", "cA1", f"e{g}")
                pt = pt_sb[:, (g % 4) * 512:(g % 4) * 512 + 512]
                em.op("tensor", lambda t, pt=pt, kc=kc, nk=nk: t.matmul(
                    o_ps[:, :], v_sb[:, kc * 128:kc * 128 + 128], pt,
                    start=(kc == 0), stop=(kc == nk - 1),
                    skip_group_check=True), "cT3", 1, f"pv{g}")
                if kc + 4 < nk:
                    mm_s(kc + 4)
            em.waitm("tensor", "cV1", f"cv{p}")
            em.op("tensor", lambda t: t.matmul(
                sum_ps[0:1, :], onesc_sb[:, :], accb_sb[:, :], start=True,
                stop=True, skip_group_check=True), "cT2", 1, f"sm{p}")
            em.waitm("tensor", "cV2", f"rc{p}")
            if p >= 1:
                em.waitm("tensor", "cA2", f"rb{p - 1}")
            em.op("tensor", lambda t: t.matmul(
                b_ps[:, :], onesr_sb[:, :], rec_sb[0:1, :], start=True,
                stop=True, skip_group_check=True), "cT4", 1, f"bm{p}")

            for kc in range(nk):
                g = chunk_g[(tm, h, kc)]
                em.waitm("scalar", "cT1", f"s{g}")
                if g >= 4:
                    em.waitm("scalar", "cT3", f"pv{g - 4}")
                    em.waitm("scalar", add_ctr[g - 4], f"ad{g - 4}")
                em.op("scalar", lambda s, g=g, h=h: s.activation(
                    out=pt_sb[:, (g % 4) * 512:(g % 4) * 512 + 512],
                    in_=s_ps[g % 4][:, :], func=AF.Exp,
                    bias=biasq_sb[:, h:h + 1], scale=SCL), "cA1", 1,
                    f"e{g}")
            em.waitm("scalar", "cT4", f"bm{p}")
            if p >= 1:
                em.waitm("scalar", "cV3", f"ae{p - 1}")
            em.op("scalar", lambda s: s.copy(recb_sb[:, :], b_ps[:, :]),
                  "cA2", 1, f"rb{p}")

            for kc in range(nk):
                g = chunk_g[(tm, h, kc)]
                pt = pt_sb[:, (g % 4) * 512:(g % 4) * 512 + 512]
                if kc >= 4 * tm:
                    j = kc - 4 * tm
                    em.waitm("vector", "cA1", f"e{g}")
                    em.op("vector", lambda v, pt=pt, j=j: v.tensor_mul(
                        pt, pt, masks_sb[:, j * 512:j * 512 + 512]), "cV1",
                        1, f"mk{g}")
                if kc % 3 == 2:
                    continue
                if kc < 4 * tm:
                    em.waitm("vector", "cA1", f"e{g}")
                if kc == 0:
                    em.op("vector", lambda v, pt=pt: v.tensor_copy(
                        acc_sb[:, :], pt), "cV1", 1, f"ad{g}")
                else:
                    em.op("vector", lambda v, pt=pt: v.tensor_add(
                        acc_sb[:, :], acc_sb[:, :], pt), "cV1", 1, f"ad{g}")
            em.waitm("vector", "cP1",
                     "ad%d" % chunk_g[(tm, h, max(k for k in range(nk)
                                                  if k % 3 == 2))])
            em.op("vector", lambda v: v.tensor_add(
                acc_sb[:, :], acc_sb[:, :], accp_sb[:, :]), "cV1", 1)
            em.op("vector", lambda v: v.tensor_copy(
                accb_sb[:, :], acc_sb[:, :]), "cV1", 1, f"cv{p}")
            em.waitm("vector", "cT2", f"sm{p}")
            if p >= 1:
                em.waitm("vector", "cT4", f"bm{p - 1}")
            em.op("vector", lambda v: v.reciprocal(rec_sb[0:1, :],
                                                   sum_ps[0:1, :]),
                  "cV2", 1, f"rc{p}")
            em.waitm("vector", "cA2", f"rb{p}")
            em.waitm("vector", "cT3", "pv%d" % chunk_g[(tm, h, nk - 1)])
            if p >= 1:
                em.waitm("vector", "cSt", f"as{p - 1}")
            em.op("vector", lambda v, p=p: v.tensor_mul(
                atst_sb[:, (p % 2) * 512:(p % 2) * 512 + 512], o_ps[:, :],
                recb_sb[:, :]), "cV3", 1, f"ae{p}")

            for kc in range(2, nk, 3):
                g = chunk_g[(tm, h, kc)]
                if kc >= 4 * tm:
                    em.waitm("gpsimd", "cV1", f"mk{g}")
                else:
                    em.waitm("gpsimd", "cA1", f"e{g}")
                pt = pt_sb[:, (g % 4) * 512:(g % 4) * 512 + 512]
                if kc == 2:
                    em.op("gpsimd", lambda gg, pt=pt: gg.tensor_copy(
                        accp_sb[:, :], pt), "cP1", 1, f"ad{g}")
                else:
                    em.op("gpsimd", lambda gg, pt=pt: gg.tensor_add(
                        accp_sb[:, :], accp_sb[:, :], pt), "cP1", 1,
                        f"ad{g}")

            em.waitm("gpsimd", "cV3", f"ae{p}")
            r0 = 512 * (tm // 2) + 128 * h
            c0 = 512 * (tm % 2)
            em.dma("gpsimd", at_slab[r0:r0 + 128, c0:c0 + 512],
                   atst_sb[:, (p % 2) * 512:(p % 2) * 512 + 512],
                   ctr="cSt", mark=f"as{p}")
        if hf == 1:
            em.waitm("gpsimd", "cSt", f"as{pair_idx[(15, 3)]}")
            em.waitm("gpsimd", "cCCo", "agwo")
            with allow_ici_all_to_all():
                em.op("gpsimd", lambda g: g.collective_compute(
                    "AllToAll", ALU.bypass,
                    replica_groups=[list(range(NC))],
                    ins=[at_slab[:, :]], outs=[at_loc[:, :]]), "cCCa", 1,
                    "a2a")

    ORD0 = [0, 2, 4, 6, 1, 3, 5, 7]
    prev_qk = {}
    qraw_prev = {}
    pprev = {}
    vprev = {}
    _last_qk = None
    _last_par = [None, None]
    _last_ps = [None, None]
    _last_v = None
    for _pos in range(16):
        _tm = (_pos // 8) * 8 + ORD0[_pos % 8]
        for _uu in range(6):
            _u = _tm * 6 + _uu
            pprev[_u] = _last_ps[_u % 2]
            _last_ps[_u % 2] = _u
            if _uu < 5:
                prev_qk[_u] = _last_qk
                _last_qk = _u
                qraw_prev[_u] = _last_par[_u % 2]
                _last_par[_u % 2] = _u
        for _j in range(4):
            _gv = 4 * _tm + _j
            vprev[_gv] = _last_v
            _last_v = _gv
        if _pos == 7:
            _last_qk = None
            _last_par = [None, None]
            _last_ps = [None, None]
            _last_v = None
    pair_idx = {}
    chunk_g = {}
    add_ctr = {}
    _p = 0
    _g = 0
    for _hf in range(2):
        for _tl in range(8):
            for _h in range(4):
                _tm = _hf * 8 + _tl
                pair_idx[(_tm, _h)] = _p
                for _kc in range(4 * _tm + 4):
                    chunk_g[(_tm, _h, _kc)] = _g
                    add_ctr[_g] = "cP1" if _kc % 3 == 2 else "cV1"
                    _g += 1
                _p += 1

    run_phase("p3a", lambda em: ph3(em, 0))
    run_phase("p3ba", lambda em: ph3bias(em, 0))
    nc.all_engine_barrier()
    run_phase("p4a", lambda em: ph4(em, 0))
    nc.all_engine_barrier()
    run_phase("p3b", lambda em: ph3(em, 1))
    run_phase("p3bb", lambda em: ph3bias(em, 1))
    nc.all_engine_barrier()
    run_phase("p4b", lambda em: ph4(em, 1))
    nc.all_engine_barrier()

    if DBG:
        def ph_dbg_a(em):
            em.dma("sync", dbg_t["dbg_k"][:, :], k_sb[:, :], ctr="cSt")
            em.dma("sync", dbg_t["dbg_v"][:, :], v_sb[:, :], ctr="cSt")
            em.dma("sync", dbg_t["dbg_qr"][:, :], qr_sb[:, :], ctr="cSt")
            em.dma("sync", dbg_t["dbg_bias"][:, :], biasq_sb[:, :], ctr="cSt",
                   mark="dbga")
            em.waitm("sync", "cSt", "dbga")
        run_phase("dbg_a", ph_dbg_a)
        nc.all_engine_barrier()
    p34s.close()

    p56s = ExitStack()
    sb5 = lambda n, sh, dt: p56s.enter_context(nc.sbuf_tensor(n, sh, dt))
    h_sb = sb5("h_sb", [128, 4 * D], F32)
    atn_sb = sb5("atn_sb", [128, 32 * 512], BF16)
    wst_sb = sb5("wst_sb", [128, 2 * 16 * 512], BF16)
    o6_sb = sb5("o6_sb", [128, 4 * D], BF16)
    x2T_sb = atn_sb
    xn2_sb = scr_bf
    y_ps = [p56s.enter_context(nc.psum_tensor(f"y_ps{i}", [128, 512], F32))
            for i in range(4)]
    tp5_ps = [p56s.enter_context(nc.psum_tensor(f"tp5_ps{i}", [128, 128],
                                                BF16)) for i in range(2)]
    dm5_ps = p56s.enter_context(nc.psum_tensor("dm5_ps", [128, 128], BF16))

    def ph56(em, mac):
        pf = f"m{mac}"
        for qt in range(4):
            em.dma("sync", h_sb[:, qt * D:(qt + 1) * D],
                   x_c[mac * 512 + qt * 128:mac * 512 + qt * 128 + 128, :],
                   ctr="cLX", mark=pf + f"xl{qt}")
        em.wait("sync", "cCCa", lambda: em.M["a2a"])
        em.dma("sync", atn_sb[:, :].rearrange("p (c n) -> p c n", n=512),
               at_loc[:, mac * 512:mac * 512 + 512].rearrange(
                   "(c p) n -> p c n", p=128), mark=pf + "atl")

        def mm_block(wg, lhs_sb, kind):
            for hs in range(2):
                k56 = (0 if kind == "o" else 16) + wg * 2 + hs
                ksem = "cLc" if k56 % 2 == 0 else "cLd"
                wsl = wst_sb[:, (k56 % 2) * 8192:(k56 % 2) * 8192 + 8192]
                em.wait("sync", "cCCo", lambda: em.M["agwo"])
                gk = k56 - 2
                if gk >= 0:
                    gwg, ghs = divmod(gk if gk < 16 else gk - 16, 2)
                    gkind = "o" if gk < 16 else "y"
                    em.waitm("sync", "cT1", pf + f"{gkind}{gwg}h{ghs}")
                if kind == "o":
                    for g4 in range(4):
                        B = 4 * hs + g4
                        srcw = comb_g[1024 * B:1024 * B + 512,
                                      wg * 512:(wg + 1) * 512].rearrange(
                            "(c p) n -> p c n", p=128)
                        dstw = wsl[:, g4 * 2048:(g4 + 1) * 2048].rearrange(
                            "p (c n) -> p c n", n=512)
                        em.dma("sync", dstw, srcw, ctr=ksem,
                               mark=(pf + f"{kind}l{wg}_{hs}")
                               if g4 == 3 else None)
                else:
                    vr = comb_g[1024 * wg + 512:1024 * wg + 1024,
                                :].rearrange("(p four) n -> p (four n)",
                                             four=4)
                    srcw = vr[:, 8192 * hs:8192 * hs + 8192].rearrange(
                        "p (dcl c) -> p dcl c", c=512)
                    dstw = wsl[:, :].rearrange("p (dcl c) -> p dcl c",
                                               c=512)
                    em.dma("sync", dstw, srcw, ctr=ksem,
                           mark=pf + f"{kind}l{wg}_{hs}")
                em.waitm("tensor", ksem, pf + f"{kind}l{wg}_{hs}")
                if kind == "o":
                    em.waitm("tensor", "cL", pf + "atl")
                else:
                    em.waitm("tensor", "cV3", pf + "te127")
                if wg == 0 and hs == 0:
                    if kind == "o":
                        pass
                    else:
                        em.waitm("tensor", "cV1", pf + "oe7_3")
                if hs == 0 and wg >= 1:
                    em.waitm("tensor", "cV1", pf + f"{kind}e{wg - 1}_3")
                for dcl in range(16):
                    dc = hs * 16 + dcl
                    for qt in range(4):
                        em.op("tensor", lambda t, dc=dc, dcl=dcl, qt=qt,
                              wsl=wsl: t.matmul(
                                  y_ps[qt][:, :],
                                  lhs_sb(dc, qt),
                                  wsl[:, dcl * 512:dcl * 512 + 512],
                                  start=(dc == 0), stop=(dc == 31),
                                  skip_group_check=True),
                              "cT1" if (dcl == 15 and qt == 3) else None, 1,
                              pf + f"{kind}{wg}h{hs}"
                              if (dcl == 15 and qt == 3) else None)
            for qt in range(4):
                em.waitm("vector", "cT1", pf + f"{kind}{wg}h1")
                em.op("vector", lambda v, qt=qt, wg=wg: v.tensor_copy(
                    o6_sb[:, qt * D + wg * 512:qt * D + wg * 512 + 512],
                    y_ps[qt][:, :]), "cV1", 1, pf + f"{kind}e{wg}_{qt}")

        for db in range(8):
            mm_block(db, lambda dc, qt: atn_sb[:, dc * 512 + qt * 128:
                                               dc * 512 + qt * 128 + 128],
                     "o")
        if DBG and mac == 1:
            em.waitm("sync", "cV1", pf + "oe7_3")
            em.dma("sync", dbg_t["dbg_o"][:, :], o6_sb[:, :], ctr="cSt")
        for qt in range(4):
            ot = o6_sb[:, qt * D:(qt + 1) * D]
            ht = h_sb[:, qt * D:(qt + 1) * D]
            em.waitm("scalar", "cV1", pf + f"oe7_{qt}")
            if qt >= 1:
                em.waitm("scalar", "cV2", pf + f"h{qt - 1}")
            ln_stats(em, ot, pf + f"l2{qt}")
            em.waitm("vector", "cA2", pf + f"l2{qt}rs")
            em.fop("vector", lambda v: v.tensor_scalar(
                va_sb[:, :], m_sb[:, :], rs_sb[:, :], -1.0, ALU.mult,
                ALU.mult), "cV2")
            em.waitm("vector", "cLX", pf + "xl3")
            em.op("vector", lambda v, ht=ht, ot=ot: v.scalar_tensor_tensor(
                ht, ot, rs_sb[:, :], ht, ALU.mult, ALU.add))
            em.op("vector", lambda v, ht=ht: v.tensor_scalar(
                ht, ht, va_sb[:, :], None, ALU.add), "cV2", 1, pf + f"h{qt}")
        if DBG and mac == 1:
            em.waitm("sync", "cV2", pf + "h3")
            em.dma("sync", dbg_t["dbg_h"][:, :], h_sb[:, :], ctr="cSt")
        for qt in range(4):
            ht = h_sb[:, qt * D:(qt + 1) * D]
            if qt == 0:
                em.waitm("scalar", "cV2", pf + "h3")
            else:
                em.waitm("scalar", "cV2", pf + f"x2{qt - 1}")
                em.waitm("scalar", "cT2", pf + f"t{qt - 1}_31")
            ln_stats(em, ht, pf + f"l3{qt}")
            em.waitm("vector", "cA2", pf + f"l3{qt}rs")
            if qt >= 1:
                em.waitm("vector", "cT2", pf + f"t{qt - 1}_31")
            em.op("vector", lambda v, ht=ht: v.tensor_scalar(
                xn2_sb[:, :], ht, m_sb[:, :], rs_sb[:, :], ALU.subtract,
                ALU.mult), "cV2", 1, pf + f"x2{qt}")
            for kc in range(32):
                g = qt * 32 + kc
                em.waitm("tensor", "cV2", pf + f"x2{qt}")
                if g >= 2:
                    em.waitm("tensor", "cV3", pf + f"te{g - 2}")
                aa, bit = kc // 2, kc % 2
                em.op("tensor", lambda t, aa=aa, bit=bit, g=g: t.transpose(
                    tp5_ps[g % 2][:, :],
                    xn2_sb[:, 256 * aa + bit:256 * aa + bit + 255:2],
                    ident_sb[:, :]), "cT2", 1, pf + f"t{qt}_{kc}")
                if kc == 31:
                    em.op("tensor", lambda t: t.transpose(
                        dm5_ps[:, :], ident_sb[:, :], ident_sb[:, :]),
                        "cT2", 1)
                em.wait("vector", "cT2",
                        (lambda k=pf + f"t{qt}_{kc}": em.M[k] + 1))
                em.op("vector", lambda v, kc=kc, qt=qt, g=g: v.tensor_copy(
                    x2T_sb[:, (kc * 4 + qt) * 128:(kc * 4 + qt) * 128 + 128],
                    tp5_ps[g % 2][:, :]), "cV3", 1, pf + f"te{g}")
        for nb in range(8):
            mm_block(nb, lambda kc, qt: x2T_sb[:, (kc * 4 + qt) * 128:
                                               (kc * 4 + qt) * 128 + 128],
                     "y")
        if DBG and mac == 1:
            em.waitm("sync", "cV1", pf + "ye7_3")
            em.dma("sync", dbg_t["dbg_y"][:, :], o6_sb[:, :], ctr="cSt")
        for qt in range(4):
            yt = o6_sb[:, qt * D:(qt + 1) * D]
            ht = h_sb[:, qt * D:(qt + 1) * D]
            em.waitm("scalar", "cV1", pf + f"ye7_{qt}")
            if qt == 0:
                em.waitm("scalar", "cV2", pf + "x23")
            else:
                em.waitm("scalar", "cV2", pf + f"ot{qt - 1}")
            ln_stats(em, yt, pf + f"l4{qt}")
            em.waitm("vector", "cA2", pf + f"l4{qt}rs")
            em.fop("vector", lambda v: v.tensor_scalar(
                va_sb[:, :], m_sb[:, :], rs_sb[:, :], -1.0, ALU.mult,
                ALU.mult), "cV2")
            em.op("vector", lambda v, ht=ht, yt=yt: v.scalar_tensor_tensor(
                ht, yt, rs_sb[:, :], ht, ALU.mult, ALU.add))
            em.op("vector", lambda v, ht=ht: v.tensor_scalar(
                ht, ht, va_sb[:, :], None, ALU.add), "cV2", 1, pf + f"ot{qt}")
            em.waitm("gpsimd", "cV2", pf + f"ot{qt}")
            em.dma("gpsimd",
                   out_c[mac * 512 + qt * 128:mac * 512 + qt * 128 + 128, :],
                   h_sb[:, qt * D:(qt + 1) * D], ctr="cSt",
                   mark=pf + f"os{qt}")
        em.waitm("gpsimd", "cSt", pf + "os3")

    run_phase("p56_0", lambda em: ph56(em, 0))
    nc.all_engine_barrier()
    run_phase("p56_1", lambda em: ph56(em, 1))

    if DBG:
        def ph_dbg_b(em):
            em.dma("sync", dbg_t["dbg_xnt"][:, :], xnT_g[0:D, :], ctr="cSt")
            em.dma("sync", dbg_t["dbg_atslab"][:, :], at_slab[:, :], ctr="cSt")
            em.dma("sync", dbg_t["dbg_atloc"][:, :], at_loc[:, :], ctr="cSt")
            em.dma("sync", dbg_t["dbg_wf"][:, :], comb_g[512:1024, :],
                   ctr="cSt")
            em.dma("sync", dbg_t["dbg_slab"][:, :], comb_slab[512:1024, :],
                   ctr="cSt")
            em.dma("sync", dbg_t["dbg_wo"][:, :], comb_g[0:512, :],
                   ctr="cSt", mark="dbgb")
            em.waitm("sync", "cSt", "dbgb")
        nc.all_engine_barrier()
        run_phase("dbg_b", ph_dbg_b)

    p56s.close()
    stack.close()
    return nc


_NC_CACHE = None


def _get_nc():
    global _NC_CACHE
    if _NC_CACHE is None:
        _NC_CACHE = build_nc()
    return _NC_CACHE


def kernel(**inputs):
    x = np.ascontiguousarray(
        np.asarray(inputs["L0_attn_hidden_states"]).reshape(S, D),
        dtype=np.float32)
    Wq = np.asarray(inputs["L0_attn_Wq"], dtype=np.float32)
    Wk = np.asarray(inputs["L0_attn_Wk"], dtype=np.float32)
    Wv = np.asarray(inputs["L0_attn_Wv"], dtype=np.float32)
    Wo = np.asarray(inputs["L0_attn_Wo"], dtype=np.float32)
    pk = np.ascontiguousarray(np.asarray(inputs["L0_ffn_weight_packed"]),
                              dtype=np.int32)
    sc = np.ascontiguousarray(np.asarray(inputs["L0_ffn_scales"]),
                              dtype=np.float16)
    def sclx(s):
        aidx = np.arange(2048)
        blk = 8 * (aidx // 128) + (aidx % 128) // 16
        return np.ascontiguousarray(s.T[blk, :])

    WqT = np.ascontiguousarray(Wq.T)
    WkT = np.ascontiguousarray(Wk.T)
    WvT = np.ascontiguousarray(Wv.T)
    WoT = np.ascontiguousarray(Wo.T)
    inv = (1.0 / 10000.0 ** (np.arange(0, HD, 2, dtype=np.float32) / HD))
    pos = np.arange(S, dtype=np.float32)
    fr = pos[:, None] * inv[None, :]
    emb = np.concatenate([fr, fr], axis=-1)
    cosT = np.ascontiguousarray(np.cos(emb).T.astype(np.float32))
    sinT = np.ascontiguousarray(np.sin(emb).T.astype(np.float32))

    in_maps = []
    for c in range(NC):
        in_maps.append({
            "x_c": np.ascontiguousarray(x[c * TPC:(c + 1) * TPC, :]),
            "wqt_s": np.ascontiguousarray(WqT[:, c * 512:(c + 1) * 512]),
            "wkt_s": np.ascontiguousarray(WkT[:, c * 128:(c + 1) * 128]),
            "wvt_s": np.ascontiguousarray(WvT[:, c * 128:(c + 1) * 128]),
            "wot_s": np.ascontiguousarray(WoT[c * 512:(c + 1) * 512, :]),
            "pkt_s": np.ascontiguousarray(pk[c * 512:(c + 1) * 512, :].T),
            "scx_s": sclx(sc[c * 512:(c + 1) * 512, :]),
            "cost": cosT,
            "sint": sinT,
        })
    nc = _get_nc()
    res = run_bass_kernel_spmd(nc, in_maps, core_ids=list(range(NC)))
    out = np.concatenate([r["out_c"] for r in res.results], axis=0)
    return out.reshape(1, S, D)


# version 18 (final)
# speedup vs baseline: 1.1263x; 1.1263x over previous
import math
import numpy as np
import ml_dtypes
import concourse.bass as bass
import concourse.mybir as mybir
from concourse.bass_utils import run_bass_kernel_spmd
from concourse.aot_env import allow_ici_all_to_all
from contextlib import ExitStack

F32 = mybir.dt.float32
BF16 = mybir.dt.bfloat16
F16 = mybir.dt.float16
I32 = mybir.dt.int32
AF = mybir.ActivationFunctionType
ALU = mybir.AluOpType
AX = mybir.AxisListType.X

S = 8192
D = 4096
HD = 128
NC = 8
TPC = S // NC
EPS = 1e-5
SCL = 1.0 / math.sqrt(HD)
LNB = -0.5 * math.log(HD)

ENGS = ("sync", "vector", "scalar", "tensor", "gpsimd")
DBG = []


class Em:

    def __init__(self, nc, sems, cnt, marks, active, engines):
        self.nc = nc
        self.sems = sems
        self.cnt = cnt
        self.M = marks
        self.active = active
        self.e = engines
        self.record = active is None

    def _mark(self, key, ctr):
        if key is None:
            return
        if self.record:
            assert key not in self.M, f"duplicate mark {key}"
            self.M[key] = self.cnt[ctr]
        else:
            assert self.M[key] == self.cnt[ctr], (
                f"mark mismatch {key}: {self.M[key]} != {self.cnt[ctr]} "
                f"(engine pass {self.active})")

    def op(self, eng, build, ctr=None, inc=1, mark=None):
        if ctr is not None:
            self.cnt[ctr] += inc
        if self.active == eng:
            bi = build(self.e[eng])
            if ctr is not None:
                bi.then_inc(self.sems[ctr], inc)
        if mark is not None:
            assert ctr is not None
            self._mark(mark, ctr)

    def dma(self, eng, out, in_, ctr="cL", mark=None):
        self.op(eng, lambda e: e.dma_start(out=out, in_=in_), ctr, 16, mark)

    def wait(self, eng, ctr, val):
        if self.active == eng:
            v = val() if callable(val) else val
            self.e[eng].wait_ge(self.sems[ctr], v)

    def waitm(self, eng, ctr, key):
        self.wait(eng, ctr, lambda: self.M[key])

    def dver(self, ctr, lkey, vkey):
        self.wait("sync", ctr, lambda: self.M[lkey])
        self.op("sync", lambda s: s.nop(), "cSy", 1, vkey)

    def fop(self, eng, build, ctr, mark=None):
        self.op(eng, build, ctr, 1, mark)
        self.wait(eng, ctr, self.cnt[ctr])


def build_nc():
    nc = bass.Bass("TRN2", target_bir_lowering=False, num_devices=NC)

    x_c = nc.dram_tensor("x_c", [TPC, D], F32, kind="ExternalInput")
    wqt_s = nc.dram_tensor("wqt_s", [D, 512], F32, kind="ExternalInput")
    wkt_s = nc.dram_tensor("wkt_s", [D, 128], F32, kind="ExternalInput")
    wvt_s = nc.dram_tensor("wvt_s", [D, 128], F32, kind="ExternalInput")
    wot_s = nc.dram_tensor("wot_s", [512, D], F32, kind="ExternalInput")
    pkt_s = nc.dram_tensor("pkt_s", [D // 2, 512], I32, kind="ExternalInput")
    scx_s = nc.dram_tensor("scx_s", [D // 2, 512], F16, kind="ExternalInput")
    cost = nc.dram_tensor("cost", [HD, S], F32, kind="ExternalInput")
    sint = nc.dram_tensor("sint", [HD, S], F32, kind="ExternalInput")
    out_c = nc.dram_tensor("out_c", [TPC, D], F32, kind="ExternalOutput")
    dbg_t = {}
    if DBG:
        for nm, shp, dt in [
                ("dbg_k", [128, S], BF16), ("dbg_v", [128, S], BF16),
                ("dbg_qr", [128, 4 * 4096], BF16),
                ("dbg_bias", [128, 4], F32),
                ("dbg_xnt", [D, TPC], BF16),
                ("dbg_atslab", [4096, TPC], BF16),
                ("dbg_atloc", [4096, TPC], BF16),
                ("dbg_wf", [512, D], BF16), ("dbg_wo", [512, D], BF16),
                ("dbg_o", [128, 4 * D], BF16), ("dbg_h", [128, 4 * D], F32),
                ("dbg_y", [128, 4 * D], BF16),
                ("dbg_slab", [512, D], BF16),
                ("dbg_wfst", [128, 32 * 512], BF16)]:
            dbg_t[nm] = nc.dram_tensor(nm, shp, dt, kind="ExternalOutput")

    xsla = nc.dram_tensor("xsla", [D, 512], BF16, kind="Internal")
    xslb = nc.dram_tensor("xslb", [D, 512], BF16, kind="Internal")
    xga = nc.dram_tensor("xga", [NC * D, 512], BF16, kind="Internal",
                         addr_space="Shared")
    xgb = nc.dram_tensor("xgb", [NC * D, 512], BF16, kind="Internal",
                         addr_space="Shared")
    at_slab = nc.dram_tensor("at_slab", [4096, TPC], BF16, kind="Internal")
    at_loc = nc.dram_tensor("at_loc", [4096, TPC], BF16, kind="Internal")
    comb_slab = nc.dram_tensor("comb_slab", [1024, D], BF16, kind="Internal")
    comb_g = nc.dram_tensor("comb_g", [NC * 1024, D], BF16, kind="Internal",
                            addr_space="Shared")
    wo_slab = comb_slab[0:512, :]
    wf_slab = comb_slab[512:1024, :].rearrange("(p four) n -> p (four n)",
                                               four=4)

    ident_d = nc.inline_tensor(np.eye(128, dtype=ml_dtypes.bfloat16),
                               name="identc")
    R = np.zeros((128, 128), np.float32)
    for i in range(64):
        R[i, 64 + i] = -1.0
        R[64 + i, i] = 1.0
    rT_d = nc.inline_tensor(R.T.copy().astype(ml_dtypes.bfloat16), name="rTc")
    onesc_d = nc.inline_tensor(np.ones((128, 1), ml_dtypes.bfloat16),
                               name="onescc")
    onesr_d = nc.inline_tensor(np.ones((1, 128), np.float32), name="onesrc")
    msk = np.zeros((128, 4 * 512), np.float32)
    ki = np.arange(128)[:, None]
    qi = np.arange(512)[None, :]
    for j in range(4):
        msk[:, j * 512:(j + 1) * 512] = (qi >= ki + 128 * j)
    masks_d = nc.inline_tensor(msk.astype(ml_dtypes.bfloat16), name="masksc")

    engines = dict(sync=nc.sync, vector=nc.vector, scalar=nc.scalar,
                   tensor=nc.tensor, gpsimd=nc.gpsimd)

    CTRS = ["cL", "cLa", "cLb", "cLc", "cLd", "cLe", "cLf", "cLX", "cSy",
            "cP1", "cSt", "cStW", "cCC",
            "cCCo", "cCCf", "cCCa", "cT1", "cT2", "cT3", "cT4",
            "cA1", "cA2", "cV1", "cV2", "cV3"]
    sems = {n: nc.alloc_semaphore(n) for n in CTRS}
    G = {n: 0 for n in CTRS}
    M = {}

    def run_phase(name, fn):
        nonlocal G
        dry = Em(nc, sems, dict(G), M, None, engines)
        fn(dry)
        with nc.Block(name) as block:
            for engname in ENGS:
                def body(e, _en=engname):
                    fn(Em(nc, sems, dict(G), M, _en, engines))
                getattr(block, engname if engname != "sync" else "sync")(body)
        G = dry.cnt

    stack = ExitStack()
    sb = lambda name, shape, dt: stack.enter_context(
        nc.sbuf_tensor(name, shape, dt))

    ident_sb = sb("ident_sb", [128, 128], BF16)
    rT_sb = sb("rT_sb", [128, 128], BF16)
    onesc_sb = sb("onesc_sb", [128, 1], BF16)
    onesr_sb = sb("onesr_sb", [1, 128], F32)
    masks_sb = sb("masks_sb", [128, 2048], BF16)
    biasq_sb = sb("biasq_sb", [128, 4], F32)
    rmq_sb = sb("rmq_sb", [1, 4], F32)
    rmk_sb = sb("rmk_sb", [1, 1], F32)
    tiny_sb = sb("tiny_sb", [1, 16], F32)
    sm_sb = sb("sm_sb", [128, 1], F32)
    ss_sb = sb("ss_sb", [128, 1], F32)
    m_sb = sb("m_sb", [128, 1], F32)
    va_sb = sb("va_sb", [128, 1], F32)
    rs_sb = sb("rs_sb", [128, 1], F32)
    scr_bf = sb("scr_bf", [128, D], BF16)
    rec_sb = sb("rec_sb", [1, 512], F32)
    recb_sb = sb("recb_sb", [128, 512], F32)
    pt_sb = sb("pt_sb", [128, 2048], BF16)
    atst_sb = sb("atst_sb", [128, 1024], BF16)
    ceps_sb = sb("ceps_sb", [128, 1], F32)
    clnb_sb = sb("clnb_sb", [128, 1], F32)

    def ph_consts(em):
        em.dma("sync", ident_sb[:, :], ident_d[:, :])
        em.dma("sync", rT_sb[:, :], rT_d[:, :])
        em.dma("sync", onesc_sb[:, :], onesc_d[:, :])
        em.dma("sync", onesr_sb[:, :], onesr_d[:, :])
        em.dma("sync", masks_sb[:, :], masks_d[:, :], mark="constsL")
        em.waitm("sync", "cL", "constsL")
        em.op("vector", lambda v: v.memset(rmq_sb[:, :], 0.0))
        em.op("vector", lambda v: v.memset(ceps_sb[:, :], EPS))
        em.op("vector", lambda v: v.memset(clnb_sb[:, :], LNB))
        em.op("vector", lambda v: v.memset(rmk_sb[:, :], 0.0), "cV1", 1,
              "rm_init")

    run_phase("consts", ph_consts)
    nc.all_engine_barrier()

    p1s = ExitStack()
    sb1 = lambda n, sh, dt: p1s.enter_context(nc.sbuf_tensor(n, sh, dt))
    xin_sb = sb1("xin_sb", [128, 2 * D], F32)
    xnb_sb = sb1("xnb_sb", [128, 2 * D], BF16)
    xnT_st = sb1("xnT_st", [128, 32 * 1024], BF16)
    tp_ps = [p1s.enter_context(nc.psum_tensor(f"tp_ps{i}", [128, 128], BF16))
             for i in range(2)]
    dm1_ps = p1s.enter_context(nc.psum_tensor("dm1_ps", [128, 128], BF16))

    def ln_stats(em, src_ap, pfx):
        em.op("scalar", lambda s: s.activation(
            out=scr_bf[:, 0:src_ap.shape[1]], in_=src_ap, func=AF.Copy,
            accum_out=sm_sb[:, :]))
        em.op("scalar", lambda s: s.activation(
            out=scr_bf[:, 0:src_ap.shape[1]], in_=src_ap, func=AF.Square,
            accum_out=ss_sb[:, :]), "cA1", 1, pfx + "st")
        em.waitm("vector", "cA1", pfx + "st")
        n = src_ap.shape[1]
        em.fop("vector", lambda v: v.tensor_scalar_mul(m_sb[:, :], sm_sb[:, :],
                                                       1.0 / n), "cV1")
        em.fop("vector", lambda v: v.tensor_scalar_mul(va_sb[:, :],
                                                       ss_sb[:, :], 1.0 / n),
               "cV1")
        em.fop("vector", lambda v: v.tensor_mul(ss_sb[:, :], m_sb[:, :],
                                                m_sb[:, :]), "cV1")
        em.fop("vector", lambda v: v.tensor_sub(va_sb[:, :], va_sb[:, :],
                                                ss_sb[:, :]), "cV1",
               pfx + "mv")
        em.waitm("scalar", "cV1", pfx + "mv")
        em.fop("scalar", lambda s: s.activation(out=va_sb[:, :],
                                                in_=va_sb[:, :], func=AF.Ln,
                                                bias=ceps_sb[:, 0:1],
                                                scale=1.0), "cA2")
        em.fop("scalar", lambda s: s.activation(out=rs_sb[:, :],
                                                in_=va_sb[:, :],
                                                func=AF.Exp, scale=-0.5),
               "cA2", pfx + "rs")

    def ph1(em):
        for tt in range(8):
            xt = xin_sb[:, (tt % 2) * D:(tt % 2 + 1) * D]
            xo = xnb_sb[:, (tt % 2) * D:(tt % 2 + 1) * D]
            if tt >= 2:
                em.waitm("sync", "cV2", f"p1ap{tt - 2}")
            em.dma("sync", xt, x_c[tt * 128:(tt + 1) * 128, :],
                   ctr="cLa" if tt % 2 == 0 else "cLb", mark=f"p1x{tt}")
            em.waitm("scalar", "cLa" if tt % 2 == 0 else "cLb", f"p1x{tt}")
            if tt >= 1:
                em.waitm("scalar", "cV2", f"p1ap{tt - 1}")
            ln_stats(em, xt, f"p1s{tt}")
            em.waitm("vector", "cA2", f"p1s{tt}rs")
            if tt >= 2:
                em.waitm("vector", "cT1", f"p1tp{tt - 2}_31")
            em.op("vector", lambda v, xo=xo, xt=xt: v.tensor_scalar(
                xo, xt, m_sb[:, :], rs_sb[:, :], ALU.subtract, ALU.mult),
                "cV2", 1, f"p1ap{tt}")
            for dc in range(32):
                g = 32 * tt + dc
                em.waitm("tensor", "cV2", f"p1ap{tt}")
                if g >= 2:
                    em.waitm("tensor", "cV3", f"p1ev{g - 2}")
                em.op("tensor", lambda t, xo=xo, dc=dc, g=g: t.transpose(
                    tp_ps[g % 2][:, :], xo[:, dc * 128:(dc + 1) * 128],
                    ident_sb[:, :]), "cT1", 1, f"p1tp{tt}_{dc}")
                if dc == 31:
                    em.op("tensor", lambda t: t.transpose(
                        dm1_ps[:, :], ident_sb[:, :], ident_sb[:, :]),
                        "cT1", 1)
                em.wait("vector", "cT1",
                        (lambda k=f"p1tp{tt}_{dc}": em.M[k] + 1))
                em.op("vector", lambda v, dc=dc, tt=tt, g=g: v.tensor_copy(
                    xnT_st[:, dc * 1024 + tt * 128:dc * 1024 + tt * 128 + 128],
                    tp_ps[g % 2][:, :]), "cV3", 1, f"p1ev{g}")
        em.waitm("gpsimd", "cV3", "p1ev127")
        for dc in range(32):
            em.dma("gpsimd", xsla[dc * 128:(dc + 1) * 128, :],
                   xnT_st[:, dc * 1024:dc * 1024 + 512], ctr="cSt",
                   mark=f"p1sa{dc}")
        em.waitm("gpsimd", "cSt", "p1sa31")
        em.op("gpsimd", lambda g: g.collective_compute(
            "AllGather", ALU.bypass, replica_groups=[list(range(NC))],
            ins=[xsla[:, :]], outs=[xga[:, :]]), "cCC", 1, "agxn1")
        em.waitm("gpsimd", "cV3", "p1ev255")
        for dc in range(32):
            em.dma("gpsimd", xslb[dc * 128:(dc + 1) * 128, :],
                   xnT_st[:, dc * 1024 + 512:(dc + 1) * 1024], ctr="cSt",
                   mark=f"p1sb{dc}")




    p2s = ExitStack()
    sb2 = lambda n, sh, dt: p2s.enter_context(nc.sbuf_tensor(n, sh, dt))
    ld32_sb = sb2("ld32_sb", [128, D], F32)
    cvt_sb = sb2("cvt_sb", [128, D], BF16)
    pki_sb = sb2("pki_sb", [128, 2 * 512], I32)
    scf_sb = sb2("scf_sb", [128, 2 * 512], F16)
    loi_sb = sb2("loi_sb", [128, 512], I32)
    hii_sb = sb2("hii_sb", [128, 512], I32)
    lof_sb = sb2("lof_sb", [128, 512], F32)
    hif_sb = sb2("hif_sb", [128, 512], F32)
    sclf_sb = sb2("sclf_sb", [128, 512], F32)
    wfT_st = sb2("wfT_st", [128, 32 * 512], BF16)

    def ph2(em):
        for ch in range(4):
            lt = ld32_sb[:, :]
            ct = cvt_sb[:, :]
            if ch >= 1:
                em.waitm("sync", "cV1", f"p2wo{ch - 1}")
            em.dma("sync", lt, wot_s[ch * 128:(ch + 1) * 128, :],
                   ctr="cLc" if ch % 2 == 0 else "cLd", mark=f"p2wl{ch}")
            em.waitm("vector", "cLc" if ch % 2 == 0 else "cLd", f"p2wl{ch}")
            if ch >= 1:
                em.wait("vector", "cStW", 16 * ch)
            em.op("vector", lambda v, ct=ct, lt=lt: v.tensor_copy(ct, lt),
                  "cV1", 1, f"p2wo{ch}")
            em.waitm("gpsimd", "cV1", f"p2wo{ch}")
            em.dma("gpsimd", wo_slab[ch * 128:(ch + 1) * 128, :], ct,
                   ctr="cStW", mark=f"p2ws{ch}")
        for aa in range(16):
            pk = pki_sb[:, (aa % 2) * 512:(aa % 2 + 1) * 512]
            sf = scf_sb[:, (aa % 2) * 512:(aa % 2 + 1) * 512]
            if aa >= 2:
                em.waitm("sync", "cV2", f"p2dq{aa - 2}")
            sem = "cLe" if aa % 2 == 0 else "cLf"
            em.dma("sync", pk, pkt_s[aa * 128:(aa + 1) * 128, :], ctr=sem)
            em.dma("sync", sf, scx_s[aa * 128:(aa + 1) * 128, :], ctr=sem,
                   mark=f"p2pl{aa}")
            em.waitm("vector", sem, f"p2pl{aa}")
            em.op("vector", lambda v, pk=pk: v.tensor_scalar(
                loi_sb[:, :], pk, 15, None, ALU.bitwise_and))
            em.op("vector", lambda v, pk=pk: v.tensor_scalar(
                hii_sb[:, :], pk, 4, 15, ALU.logical_shift_right,
                ALU.bitwise_and))
            em.op("vector", lambda v: v.tensor_copy(lof_sb[:, :],
                                                    loi_sb[:, :]))
            em.op("vector", lambda v: v.tensor_copy(hif_sb[:, :],
                                                    hii_sb[:, :]))
            em.op("vector", lambda v, sf=sf: v.tensor_copy(sclf_sb[:, :], sf))
            em.op("vector", lambda v, aa=aa: v.scalar_tensor_tensor(
                wfT_st[:, (2 * aa) * 512:(2 * aa) * 512 + 512],
                lof_sb[:, :], -8.0, sclf_sb[:, :], ALU.add, ALU.mult))
            em.op("vector", lambda v, aa=aa: v.scalar_tensor_tensor(
                wfT_st[:, (2 * aa + 1) * 512:(2 * aa + 1) * 512 + 512],
                hif_sb[:, :], -8.0, sclf_sb[:, :], ALU.add, ALU.mult),
                "cV2", 1, f"p2dq{aa}")
        em.waitm("gpsimd", "cV2", "p2dq15")
        for kc in range(32):
            em.dma("gpsimd", wf_slab[:, kc * 512:(kc + 1) * 512],
                   wfT_st[:, kc * 512:(kc + 1) * 512], ctr="cSt",
                   mark=f"p2fs{kc}")
        if DBG:
            em.waitm("sync", "cV2", "p2dq15")
            em.dma("sync", dbg_t["dbg_wfst"][:, :], wfT_st[:, :], ctr="cSt")

    run_phase("p1", lambda em: (ph1(em), ph2(em)))
    nc.all_engine_barrier()
    p2s.close()
    p1s.close()

    p34s = ExitStack()
    sb3 = lambda n, sh, dt: p34s.enter_context(nc.sbuf_tensor(n, sh, dt))
    wq_sb = sb3("wq_sb", [128, 32 * 512], BF16)
    wk_sb = sb3("wk_sb", [128, 32 * 128], BF16)
    wv_sb = sb3("wv_sb", [128, 32 * 128], BF16)
    k_sb = sb3("k_sb", [128, S], BF16)
    v_sb = sb3("v_sb", [128, S], BF16)
    qr_sb = sb3("qr_sb", [128, 4 * 4096], BF16)
    cosb_sb = sb3("cosb_sb", [128, 4096], BF16)
    sinb_sb = sb3("sinb_sb", [128, 4096], BF16)
    xnT_sb = sb3("xnT_sb", [128, 32 * 512], BF16)
    wld_sb = sb3("wld_sb", [128, 4096], F32)
    acc_sb = wld_sb[:, 0:512]
    accp_sb = wld_sb[:, 512:1024]
    qraw_sb = sb3("qraw_sb", [128, 2 * 512], BF16)
    t1_sb = pt_sb[:, 1024:1536]
    t2_sb = pt_sb[:, 1536:2048]
    sq_sb = sb3("sq_sb", [128, 512], BF16)
    accb_sb = sq_sb
    vtmp_sb = sb3("vtmp_sb", [128, 2 * 512], BF16)
    q_ps = [p34s.enter_context(nc.psum_tensor(f"q_ps{i}", [128, 512], F32))
            for i in range(2)]
    sx_ps = [p34s.enter_context(nc.psum_tensor(f"sx_ps{i}", [128, 512], F32))
             for i in range(2)]
    rot_ps = p34s.enter_context(nc.psum_tensor("rot_ps", [128, 512], F32))
    n_ps = p34s.enter_context(nc.psum_tensor("n_ps", [1, 512], F32))
    o_ps = p34s.enter_context(nc.psum_tensor("o_ps", [128, 512], F32))
    tp3_ps0 = p34s.enter_context(nc.psum_tensor("tp3_ps0", [128, 128],
                                                 BF16))
    tp3_ps = [tp3_ps0, tp3_ps0]
    s_ps = q_ps + sx_ps
    sum_ps = n_ps
    b_ps = rot_ps

    def ph3(em, hf):
        if hf == 0:
            em.waitm("gpsimd", "cSt", "p1sb31")
            em.waitm("gpsimd", "cCC", "agxn1")
            em.op("gpsimd", lambda g: g.collective_compute(
                "AllGather", ALU.bypass, replica_groups=[list(range(NC))],
                ins=[xslb[:, :]], outs=[xgb[:, :]]), "cCC", 1, "agxn2")
            em.waitm("gpsimd", "cSt", "p2fs31")
            em.waitm("gpsimd", "cStW", "p2ws3")
            em.waitm("gpsimd", "cCC", "agxn2")
            em.op("gpsimd", lambda g: g.collective_compute(
                "AllGather", ALU.bypass, replica_groups=[list(range(NC))],
                ins=[comb_slab[:, :]], outs=[comb_g[:, :]]), "cCCo", 1,
                "agwo")
        if hf == 0:
            em.waitm("sync", "cSt", "p2fs31")
            em.waitm("sync", "cStW", "p2ws3")
            for sd in range(4):
                if sd >= 1:
                    em.waitm("sync", "cV2", f"wqcv{sd - 1}")
                em.dma("sync",
                       wld_sb[:, :].rearrange("p (c n) -> p c n", n=512),
                       wqt_s[sd * 1024:(sd + 1) * 1024, :].rearrange(
                           "(c p) n -> p c n", p=128), mark=f"wql{sd}")
                em.dver("cL", f"wql{sd}", f"wqv{sd}")
                em.waitm("vector", "cSy", f"wqv{sd}")
                em.op("vector", lambda v, sd=sd: v.tensor_copy(
                    wq_sb[:, sd * 4096:(sd + 1) * 4096], wld_sb[:, :]),
                    "cV2", 1, f"wqcv{sd}")
        pf = f"h{hf}"
        em.waitm("sync", "cV2", "wqcv3" if hf == 0 else "h0sincv")
        em.dma("sync", wld_sb[:, :], cost[:, hf * 4096:(hf + 1) * 4096],
               mark=pf + "cosl")
        em.dver("cL", pf + "cosl", pf + "cosv")
        em.waitm("vector", "cSy", pf + "cosv")
        em.op("vector", lambda v: v.tensor_copy(cosb_sb[:, :], wld_sb[:, :]),
              "cV2", 1, pf + "coscv")
        em.waitm("sync", "cV2", pf + "coscv")
        em.dma("sync", wld_sb[:, :], sint[:, hf * 4096:(hf + 1) * 4096],
               mark=pf + "sinl")
        em.dver("cL", pf + "sinl", pf + "sinv")
        em.waitm("vector", "cSy", pf + "sinv")
        em.op("vector", lambda v: v.tensor_copy(sinb_sb[:, :], wld_sb[:, :]),
              "cV2", 1, pf + "sincv")
        if hf == 0:
            for nm, tgt, dst in (("wk", wkt_s, wk_sb), ("wv", wvt_s, wv_sb)):
                em.waitm("sync", "cV2",
                         pf + "sincv" if nm == "wk" else "wkcv")
                em.dma("sync",
                       wld_sb[:, :].rearrange("p (c n) -> p c n", n=128),
                       tgt.rearrange("(c p) n -> p c n", p=128),
                       mark=nm + "l")
                em.dver("cL", nm + "l", nm + "v")
                em.waitm("vector", "cSy", nm + "v")
                em.op("vector", lambda v, dst=dst: v.tensor_copy(
                    dst[:, :], wld_sb[:, :]), "cV2", 1, nm + "cv")
        for idx, tl in enumerate(ORD0):
            tm = hf * 8 + tl
            if idx >= 1:
                ptm = hf * 8 + ORD0[idx - 1]
                em.waitm("sync", "cT1", f"u{ptm * 6 + 5}mm")
            elif hf == 1:
                em.waitm("sync", "cT1", f"u{(ORD0[-1]) * 6 + 5}mm")
            sl, c0 = tm // 2, (tm % 2) * 512
            em.wait("sync", "cCC",
                    lambda c0=c0: em.M["agxn1" if c0 == 0 else "agxn2"])
            xsrc = xga if c0 == 0 else xgb
            em.dma("sync",
                   xnT_sb[:, :].rearrange("p (c n) -> p c n", n=512),
                   xsrc[sl * D:(sl + 1) * D, :].rearrange(
                       "(c p) n -> p c n", p=128), mark=f"xnl{tm}")
            em.dver("cL", f"xnl{tm}", f"xnv{tm}")
            for uu in range(6):
                u = tm * 6 + uu
                qrw = qraw_sb[:, (u % 2) * 512:(u % 2) * 512 + 512] \
                    if uu < 5 else vtmp_sb[:, (tm % 2) * 512:(tm % 2) * 512
                                           + 512]
                em.waitm("tensor", "cSy", f"xnv{tm}")
                if pprev.get(u) is not None:
                    em.waitm("tensor", "cA1", f"u{pprev[u]}ev")
                for dc in range(32):
                    if uu < 4:
                        lhs = wq_sb[:, dc * 512 + uu * 128:
                                    dc * 512 + uu * 128 + 128]
                    elif uu == 4:
                        lhs = wk_sb[:, dc * 128:dc * 128 + 128]
                    else:
                        lhs = wv_sb[:, dc * 128:dc * 128 + 128]
                    em.op("tensor", lambda t, lhs=lhs, dc=dc, u=u: t.matmul(
                        q_ps[u % 2][:, :], lhs,
                        xnT_sb[:, dc * 512:dc * 512 + 512],
                        start=(dc == 0), stop=(dc == 31)),
                        "cT1" if dc == 31 else None, 1,
                        f"u{u}mm" if dc == 31 else None)
                em.waitm("scalar", "cT1", f"u{u}mm")
                if uu < 5 and qraw_prev.get(u) is not None:
                    em.waitm("scalar", "cV1", f"u{qraw_prev[u]}rp")
                if uu == 5 and tm >= 2:
                    em.waitm("scalar", "cT4", f"u{(tm - 2) * 6 + 5}tp3")
                em.op("scalar", lambda s, qrw=qrw, u=u: s.copy(
                    qrw, q_ps[u % 2][:, :]), "cA1", 1, f"u{u}ev")
                if uu < 5:
                    em.waitm("tensor", "cA1", f"u{u}ev")
                    qk_prev = prev_qk.get(u)
                    if qk_prev is not None:
                        em.waitm("tensor", "cV1", f"u{qk_prev}rp")
                    em.op("tensor", lambda t, qrw=qrw: t.matmul(
                        rot_ps[:, :], rT_sb[:, :], qrw, start=True,
                        stop=True), "cT2", 1, f"u{u}rot")
                    em.waitm("vector", "cT2", f"u{u}rot")
                    if uu < 4:
                        dst = qr_sb[:, uu * 4096 + tl * 512:
                                    uu * 4096 + tl * 512 + 512]
                    else:
                        dst = k_sb[:, tm * 512:tm * 512 + 512]
                    co = cosb_sb[:, tl * 512:tl * 512 + 512]
                    si = sinb_sb[:, tl * 512:tl * 512 + 512]
                    em.op("vector", lambda v, qrw=qrw, co=co: v.tensor_mul(
                        t1_sb[:, :], qrw, co))
                    em.op("vector", lambda v, si=si: v.tensor_mul(
                        t2_sb[:, :], rot_ps[:, :], si))
                    em.op("vector", lambda v, dst=dst: v.tensor_add(
                        dst, t1_sb[:, :], t2_sb[:, :]), "cV1", 1, f"u{u}rp")
                    em.waitm("scalar", "cV1", f"u{u}rp")
                    if qk_prev is not None:
                        em.waitm("scalar", "cT3", f"u{qk_prev}nm")
                    em.op("scalar", lambda s, dst=dst: s.activation(
                        out=sq_sb[:, :], in_=dst, func=AF.Square),
                        "cA2", 1, f"u{u}sq")
                    em.waitm("tensor", "cA2", f"u{u}sq")
                    if qk_prev is not None:
                        em.waitm("tensor", "cV2", f"u{qk_prev}nr")
                    em.op("tensor", lambda t: t.matmul(
                        n_ps[0:1, :], onesc_sb[:, :], sq_sb[:, :],
                        start=True, stop=True), "cT3", 1, f"u{u}nm")
                    em.waitm("vector", "cT3", f"u{u}nm")
                    em.fop("vector", lambda v: v.reduce_max(
                        tiny_sb[:, 0:1], n_ps[0:1, :], axis=AX), "cV2")
                    tgt = (rmq_sb[0:1, uu:uu + 1] if uu < 4
                           else rmk_sb[0:1, 0:1])
                    em.fop("vector", lambda v, tgt=tgt: v.tensor_max(
                        tgt, tgt, tiny_sb[:, 0:1]), "cV2", f"u{u}nr")
                else:
                    em.waitm("tensor", "cA1", f"u{u}ev")
                    for j in range(4):
                        g = 4 * tm + j
                        if vprev.get(g) is not None:
                            em.waitm("tensor", "cV3", f"vev{vprev[g]}")
                        em.op("tensor", lambda t, qrw=qrw, j=j, g=g:
                              t.transpose(tp3_ps[g % 2][:, :],
                                          qrw[:, j * 128:j * 128 + 128],
                                          ident_sb[:, :]),
                              "cT4", 1,
                              f"u{u}tp3" if j == 3 else f"u{u}tp{j}")
                        em.wait("vector", "cT4",
                                (lambda k=(f"u{u}tp3" if j == 3
                                           else f"u{u}tp{j}"): em.M[k]))
                        em.op("vector", lambda v, tm=tm, j=j, g=g:
                              v.tensor_copy(
                                  v_sb[:, (tm * 4 + j) * 128:
                                       (tm * 4 + j) * 128 + 128],
                                  tp3_ps[g % 2][:, :]), "cV3", 1, f"vev{g}")

    def ph3bias(em, hf):
        em.waitm("vector", "cV2", f"u{(hf * 8 + 7) * 6 + 4}nr")
        for h in range(4):
            if h >= 1:
                em.waitm("vector", "cA2", f"b{hf}_{h - 1}B")
            em.fop("vector", lambda v, h=h: v.tensor_mul(
                tiny_sb[:, 1:2], rmq_sb[0:1, h:h + 1], rmk_sb[0:1, 0:1]),
                "cV1", f"b{hf}_{h}mul")
            em.waitm("scalar", "cV1", f"b{hf}_{h}mul")
            em.fop("scalar", lambda s: s.activation(
                out=tiny_sb[:, 3:4], in_=tiny_sb[:, 1:2], func=AF.Ln), "cA2")
            em.fop("scalar", lambda s: s.activation(
                out=tiny_sb[:, 3:4], in_=tiny_sb[:, 3:4], func=AF.Exp,
                scale=0.5, bias=clnb_sb[0:1, 0:1]), "cA2", f"b{hf}_{h}B")
            em.waitm("vector", "cA2", f"b{hf}_{h}B")
            if h >= 1:
                em.waitm("vector", "cT2", f"b{hf}_{h - 1}mm")
            em.fop("vector", lambda v: v.tensor_scalar_mul(
                tiny_sb[:, 2:3], tiny_sb[:, 3:4], -1.0), "cV2",
                f"b{hf}_{h}ng")
            em.waitm("tensor", "cV2", f"b{hf}_{h}ng")
            if h == 0:
                em.waitm("tensor", "cV1", f"u{(hf * 8 + 7) * 6 + 4}rp")
            if h >= 1:
                em.waitm("tensor", "cA1", f"b{hf}_{h - 1}cp")
            em.op("tensor", lambda t: t.matmul(
                rot_ps[:, 0:1], onesr_sb[:, :], tiny_sb[0:1, 2:3],
                start=True, stop=True), "cT2", 1, f"b{hf}_{h}mm")
            em.waitm("scalar", "cT2", f"b{hf}_{h}mm")
            em.op("scalar", lambda s, h=h: s.copy(
                biasq_sb[:, h:h + 1], rot_ps[:, 0:1]), "cA1", 1,
                f"b{hf}_{h}cp")
        em.wait("vector", "cA2", lambda: em.M[f"b{hf}_3B"])
        em.op("vector", lambda v: v.memset(rmq_sb[:, :], 0.0))
        em.op("vector", lambda v: v.memset(rmk_sb[:, :], 0.0), "cV3", 1,
              f"b{hf}rst")

    def ph4(em, hf):
        pairs = [(hf * 8 + tl, h) for tl in range(8) for h in range(4)]
        for (tm, h) in pairs:
            p = pair_idx[(tm, h)]
            nk = 4 * tm + 4
            tl = tm % 8
            qrh = qr_sb[:, h * 4096 + tl * 512:h * 4096 + tl * 512 + 512]

            def mm_s(kc):
                g = chunk_g[(tm, h, kc)]
                if g >= 4:
                    em.waitm("tensor", "cA1", f"e{g - 4}")
                em.op("tensor", lambda t, kc=kc, g=g, qrh=qrh: t.matmul(
                    s_ps[g % 4][:, :], k_sb[:, kc * 128:kc * 128 + 128],
                    qrh, start=True, stop=True, skip_group_check=True),
                    "cT1", 1, f"s{g}")

            if p >= 1:
                em.waitm("tensor", "cV3", f"ae{p - 1}")
                em.waitm("tensor", "cV2", f"rc{p - 1}")
            for kk in range(min(4, nk)):
                mm_s(kk)
            for kc in range(nk):
                g = chunk_g[(tm, h, kc)]
                if kc >= 4 * tm:
                    em.waitm("tensor", "cV1", f"mk{g}")
                else:
                    em.waitm("tensor", "cA1", f"e{g}")
                pt = pt_sb[:, (g % 4) * 512:(g % 4) * 512 + 512]
                em.op("tensor", lambda t, pt=pt, kc=kc, nk=nk: t.matmul(
                    o_ps[:, :], v_sb[:, kc * 128:kc * 128 + 128], pt,
                    start=(kc == 0), stop=(kc == nk - 1),
                    skip_group_check=True), "cT3", 1, f"pv{g}")
                if kc + 4 < nk:
                    mm_s(kc + 4)
            em.waitm("tensor", "cV1", f"cv{p}")
            em.op("tensor", lambda t: t.matmul(
                sum_ps[0:1, :], onesc_sb[:, :], accb_sb[:, :], start=True,
                stop=True, skip_group_check=True), "cT2", 1, f"sm{p}")
            em.waitm("tensor", "cV2", f"rc{p}")
            if p >= 1:
                em.waitm("tensor", "cA2", f"rb{p - 1}")
            em.op("tensor", lambda t: t.matmul(
                b_ps[:, :], onesr_sb[:, :], rec_sb[0:1, :], start=True,
                stop=True, skip_group_check=True), "cT4", 1, f"bm{p}")

            for kc in range(nk):
                g = chunk_g[(tm, h, kc)]
                em.waitm("scalar", "cT1", f"s{g}")
                if g >= 4:
                    em.waitm("scalar", "cT3", f"pv{g - 4}")
                    em.waitm("scalar", add_ctr[g - 4], f"ad{g - 4}")
                em.op("scalar", lambda s, g=g, h=h: s.activation(
                    out=pt_sb[:, (g % 4) * 512:(g % 4) * 512 + 512],
                    in_=s_ps[g % 4][:, :], func=AF.Exp,
                    bias=biasq_sb[:, h:h + 1], scale=SCL), "cA1", 1,
                    f"e{g}")
            em.waitm("scalar", "cT4", f"bm{p}")
            if p >= 1:
                em.waitm("scalar", "cV3", f"ae{p - 1}")
            em.op("scalar", lambda s: s.copy(recb_sb[:, :], b_ps[:, :]),
                  "cA2", 1, f"rb{p}")

            for kc in range(nk):
                g = chunk_g[(tm, h, kc)]
                pt = pt_sb[:, (g % 4) * 512:(g % 4) * 512 + 512]
                if kc >= 4 * tm:
                    j = kc - 4 * tm
                    em.waitm("vector", "cA1", f"e{g}")
                    em.op("vector", lambda v, pt=pt, j=j: v.tensor_mul(
                        pt, pt, masks_sb[:, j * 512:j * 512 + 512]), "cV1",
                        1, f"mk{g}")
                if kc % 3 == 2:
                    continue
                if kc < 4 * tm:
                    em.waitm("vector", "cA1", f"e{g}")
                if kc == 0:
                    em.op("vector", lambda v, pt=pt: v.tensor_copy(
                        acc_sb[:, :], pt), "cV1", 1, f"ad{g}")
                else:
                    em.op("vector", lambda v, pt=pt: v.tensor_add(
                        acc_sb[:, :], acc_sb[:, :], pt), "cV1", 1, f"ad{g}")
            em.waitm("vector", "cP1",
                     "ad%d" % chunk_g[(tm, h, max(k for k in range(nk)
                                                  if k % 3 == 2))])
            em.op("vector", lambda v: v.tensor_add(
                acc_sb[:, :], acc_sb[:, :], accp_sb[:, :]), "cV1", 1)
            em.op("vector", lambda v: v.tensor_copy(
                accb_sb[:, :], acc_sb[:, :]), "cV1", 1, f"cv{p}")
            em.waitm("vector", "cT2", f"sm{p}")
            if p >= 1:
                em.waitm("vector", "cT4", f"bm{p - 1}")
            em.op("vector", lambda v: v.reciprocal(rec_sb[0:1, :],
                                                   sum_ps[0:1, :]),
                  "cV2", 1, f"rc{p}")
            em.waitm("vector", "cA2", f"rb{p}")
            em.waitm("vector", "cT3", "pv%d" % chunk_g[(tm, h, nk - 1)])
            if p >= 1:
                em.waitm("vector", "cSt", f"as{p - 1}")
            em.op("vector", lambda v, p=p: v.tensor_mul(
                atst_sb[:, (p % 2) * 512:(p % 2) * 512 + 512], o_ps[:, :],
                recb_sb[:, :]), "cV3", 1, f"ae{p}")

            for kc in range(2, nk, 3):
                g = chunk_g[(tm, h, kc)]
                if kc >= 4 * tm:
                    em.waitm("gpsimd", "cV1", f"mk{g}")
                else:
                    em.waitm("gpsimd", "cA1", f"e{g}")
                pt = pt_sb[:, (g % 4) * 512:(g % 4) * 512 + 512]
                if kc == 2:
                    em.op("gpsimd", lambda gg, pt=pt: gg.tensor_copy(
                        accp_sb[:, :], pt), "cP1", 1, f"ad{g}")
                else:
                    em.op("gpsimd", lambda gg, pt=pt: gg.tensor_add(
                        accp_sb[:, :], accp_sb[:, :], pt), "cP1", 1,
                        f"ad{g}")

            em.waitm("gpsimd", "cV3", f"ae{p}")
            r0 = 512 * (tm // 2) + 128 * h
            c0 = 512 * (tm % 2)
            em.dma("gpsimd", at_slab[r0:r0 + 128, c0:c0 + 512],
                   atst_sb[:, (p % 2) * 512:(p % 2) * 512 + 512],
                   ctr="cSt", mark=f"as{p}")
        if hf == 1:
            em.waitm("gpsimd", "cSt", f"as{pair_idx[(15, 3)]}")
            em.waitm("gpsimd", "cCCo", "agwo")
            with allow_ici_all_to_all():
                em.op("gpsimd", lambda g: g.collective_compute(
                    "AllToAll", ALU.bypass,
                    replica_groups=[list(range(NC))],
                    ins=[at_slab[:, :]], outs=[at_loc[:, :]]), "cCCa", 1,
                    "a2a")

    ORD0 = [0, 2, 4, 6, 1, 3, 5, 7]
    prev_qk = {}
    qraw_prev = {}
    pprev = {}
    vprev = {}
    _last_qk = None
    _last_par = [None, None]
    _last_ps = [None, None]
    _last_v = None
    for _pos in range(16):
        _tm = (_pos // 8) * 8 + ORD0[_pos % 8]
        for _uu in range(6):
            _u = _tm * 6 + _uu
            pprev[_u] = _last_ps[_u % 2]
            _last_ps[_u % 2] = _u
            if _uu < 5:
                prev_qk[_u] = _last_qk
                _last_qk = _u
                qraw_prev[_u] = _last_par[_u % 2]
                _last_par[_u % 2] = _u
        for _j in range(4):
            _gv = 4 * _tm + _j
            vprev[_gv] = _last_v
            _last_v = _gv
        if _pos == 7:
            _last_qk = None
            _last_par = [None, None]
            _last_ps = [None, None]
            _last_v = None
    pair_idx = {}
    chunk_g = {}
    add_ctr = {}
    _p = 0
    _g = 0
    for _hf in range(2):
        for _tl in range(8):
            for _h in range(4):
                _tm = _hf * 8 + _tl
                pair_idx[(_tm, _h)] = _p
                for _kc in range(4 * _tm + 4):
                    chunk_g[(_tm, _h, _kc)] = _g
                    add_ctr[_g] = "cP1" if _kc % 3 == 2 else "cV1"
                    _g += 1
                _p += 1

    run_phase("p3a", lambda em: ph3(em, 0))
    run_phase("p3ba", lambda em: ph3bias(em, 0))
    nc.all_engine_barrier()
    run_phase("p4a", lambda em: ph4(em, 0))
    nc.all_engine_barrier()
    run_phase("p3b", lambda em: ph3(em, 1))
    run_phase("p3bb", lambda em: ph3bias(em, 1))
    nc.all_engine_barrier()
    run_phase("p4b", lambda em: ph4(em, 1))
    nc.all_engine_barrier()

    if DBG:
        def ph_dbg_a(em):
            em.dma("sync", dbg_t["dbg_k"][:, :], k_sb[:, :], ctr="cSt")
            em.dma("sync", dbg_t["dbg_v"][:, :], v_sb[:, :], ctr="cSt")
            em.dma("sync", dbg_t["dbg_qr"][:, :], qr_sb[:, :], ctr="cSt")
            em.dma("sync", dbg_t["dbg_bias"][:, :], biasq_sb[:, :], ctr="cSt",
                   mark="dbga")
            em.waitm("sync", "cSt", "dbga")
        run_phase("dbg_a", ph_dbg_a)
        nc.all_engine_barrier()
    p34s.close()

    p56s = ExitStack()
    sb5 = lambda n, sh, dt: p56s.enter_context(nc.sbuf_tensor(n, sh, dt))
    h_sb = sb5("h_sb", [128, 4 * D], F32)
    atn_sb = sb5("atn_sb", [128, 32 * 512], BF16)
    wst_sb = sb5("wst_sb", [128, 2 * 16 * 512], BF16)
    o6_sb = sb5("o6_sb", [128, 4 * D], BF16)
    x2T_sb = atn_sb
    xn2_sb = scr_bf
    y_ps = [p56s.enter_context(nc.psum_tensor(f"y_ps{i}", [128, 512], F32))
            for i in range(4)]
    tp5_ps = [p56s.enter_context(nc.psum_tensor(f"tp5_ps{i}", [128, 128],
                                                BF16)) for i in range(2)]
    dm5_ps = p56s.enter_context(nc.psum_tensor("dm5_ps", [128, 128], BF16))

    def ph56(em, mac):
        pf = f"m{mac}"
        for qt in range(4):
            em.dma("sync", h_sb[:, qt * D:(qt + 1) * D],
                   x_c[mac * 512 + qt * 128:mac * 512 + qt * 128 + 128, :],
                   ctr="cLX", mark=pf + f"xl{qt}")
        em.wait("sync", "cCCa", lambda: em.M["a2a"])
        em.dma("sync", atn_sb[:, :].rearrange("p (c n) -> p c n", n=512),
               at_loc[:, mac * 512:mac * 512 + 512].rearrange(
                   "(c p) n -> p c n", p=128), mark=pf + "atl")

        def mm_block(wg, lhs_sb, kind):
            for hs in range(2):
                k56 = (0 if kind == "o" else 16) + wg * 2 + hs
                ksem = "cLc" if k56 % 2 == 0 else "cLd"
                wsl = wst_sb[:, (k56 % 2) * 8192:(k56 % 2) * 8192 + 8192]
                em.wait("sync", "cCCo", lambda: em.M["agwo"])
                gk = k56 - 2
                if gk >= 0:
                    gwg, ghs = divmod(gk if gk < 16 else gk - 16, 2)
                    gkind = "o" if gk < 16 else "y"
                    em.waitm("sync", "cT1", pf + f"{gkind}{gwg}h{ghs}")
                if kind == "o":
                    for g4 in range(4):
                        B = 4 * hs + g4
                        srcw = comb_g[1024 * B:1024 * B + 512,
                                      wg * 512:(wg + 1) * 512].rearrange(
                            "(c p) n -> p c n", p=128)
                        dstw = wsl[:, g4 * 2048:(g4 + 1) * 2048].rearrange(
                            "p (c n) -> p c n", n=512)
                        em.dma("sync", dstw, srcw, ctr=ksem,
                               mark=(pf + f"{kind}l{wg}_{hs}")
                               if g4 == 3 else None)
                else:
                    vr = comb_g[1024 * wg + 512:1024 * wg + 1024,
                                :].rearrange("(p four) n -> p (four n)",
                                             four=4)
                    srcw = vr[:, 8192 * hs:8192 * hs + 8192].rearrange(
                        "p (dcl c) -> p dcl c", c=512)
                    dstw = wsl[:, :].rearrange("p (dcl c) -> p dcl c",
                                               c=512)
                    em.dma("sync", dstw, srcw, ctr=ksem,
                           mark=pf + f"{kind}l{wg}_{hs}")
                em.waitm("tensor", ksem, pf + f"{kind}l{wg}_{hs}")
                if kind == "o":
                    em.waitm("tensor", "cL", pf + "atl")
                else:
                    em.waitm("tensor", "cV3", pf + "te127")
                if wg == 0 and hs == 0:
                    if kind == "o":
                        pass
                    else:
                        em.waitm("tensor", "cV1", pf + "oe7_3")
                if hs == 0 and wg >= 1:
                    em.waitm("tensor", "cV1", pf + f"{kind}e{wg - 1}_3")
                for dcl in range(16):
                    dc = hs * 16 + dcl
                    for qt in range(4):
                        em.op("tensor", lambda t, dc=dc, dcl=dcl, qt=qt,
                              wsl=wsl: t.matmul(
                                  y_ps[qt][:, :],
                                  lhs_sb(dc, qt),
                                  wsl[:, dcl * 512:dcl * 512 + 512],
                                  start=(dc == 0), stop=(dc == 31),
                                  skip_group_check=True),
                              "cT1" if (dcl == 15 and qt == 3) else None, 1,
                              pf + f"{kind}{wg}h{hs}"
                              if (dcl == 15 and qt == 3) else None)
            for qt in range(4):
                em.waitm("vector", "cT1", pf + f"{kind}{wg}h1")
                em.op("vector", lambda v, qt=qt, wg=wg: v.tensor_copy(
                    o6_sb[:, qt * D + wg * 512:qt * D + wg * 512 + 512],
                    y_ps[qt][:, :]), "cV1", 1, pf + f"{kind}e{wg}_{qt}")

        for db in range(8):
            mm_block(db, lambda dc, qt: atn_sb[:, dc * 512 + qt * 128:
                                               dc * 512 + qt * 128 + 128],
                     "o")
        if DBG and mac == 1:
            em.waitm("sync", "cV1", pf + "oe7_3")
            em.dma("sync", dbg_t["dbg_o"][:, :], o6_sb[:, :], ctr="cSt")
        for qt in range(4):
            ot = o6_sb[:, qt * D:(qt + 1) * D]
            ht = h_sb[:, qt * D:(qt + 1) * D]
            em.waitm("scalar", "cV1", pf + f"oe7_{qt}")
            if qt >= 1:
                em.waitm("scalar", "cV2", pf + f"h{qt - 1}")
            ln_stats(em, ot, pf + f"l2{qt}")
            em.waitm("vector", "cA2", pf + f"l2{qt}rs")
            em.fop("vector", lambda v: v.tensor_scalar(
                va_sb[:, :], m_sb[:, :], rs_sb[:, :], -1.0, ALU.mult,
                ALU.mult), "cV2")
            em.waitm("vector", "cLX", pf + "xl3")
            em.op("vector", lambda v, ht=ht, ot=ot: v.scalar_tensor_tensor(
                ht, ot, rs_sb[:, :], ht, ALU.mult, ALU.add))
            em.op("vector", lambda v, ht=ht: v.tensor_scalar(
                ht, ht, va_sb[:, :], None, ALU.add), "cV2", 1, pf + f"h{qt}")
        if DBG and mac == 1:
            em.waitm("sync", "cV2", pf + "h3")
            em.dma("sync", dbg_t["dbg_h"][:, :], h_sb[:, :], ctr="cSt")
        for qt in range(4):
            ht = h_sb[:, qt * D:(qt + 1) * D]
            if qt == 0:
                em.waitm("scalar", "cV2", pf + "h3")
            else:
                em.waitm("scalar", "cV2", pf + f"x2{qt - 1}")
                em.waitm("scalar", "cT2", pf + f"t{qt - 1}_31")
            ln_stats(em, ht, pf + f"l3{qt}")
            em.waitm("vector", "cA2", pf + f"l3{qt}rs")
            if qt >= 1:
                em.waitm("vector", "cT2", pf + f"t{qt - 1}_31")
            em.op("vector", lambda v, ht=ht: v.tensor_scalar(
                xn2_sb[:, :], ht, m_sb[:, :], rs_sb[:, :], ALU.subtract,
                ALU.mult), "cV2", 1, pf + f"x2{qt}")
            for kc in range(32):
                g = qt * 32 + kc
                em.waitm("tensor", "cV2", pf + f"x2{qt}")
                if g >= 2:
                    em.waitm("tensor", "cV3", pf + f"te{g - 2}")
                aa, bit = kc // 2, kc % 2
                em.op("tensor", lambda t, aa=aa, bit=bit, g=g: t.transpose(
                    tp5_ps[g % 2][:, :],
                    xn2_sb[:, 256 * aa + bit:256 * aa + bit + 255:2],
                    ident_sb[:, :]), "cT2", 1, pf + f"t{qt}_{kc}")
                if kc == 31:
                    em.op("tensor", lambda t: t.transpose(
                        dm5_ps[:, :], ident_sb[:, :], ident_sb[:, :]),
                        "cT2", 1)
                em.wait("vector", "cT2",
                        (lambda k=pf + f"t{qt}_{kc}": em.M[k] + 1))
                em.op("vector", lambda v, kc=kc, qt=qt, g=g: v.tensor_copy(
                    x2T_sb[:, (kc * 4 + qt) * 128:(kc * 4 + qt) * 128 + 128],
                    tp5_ps[g % 2][:, :]), "cV3", 1, pf + f"te{g}")
        for nb in range(8):
            mm_block(nb, lambda kc, qt: x2T_sb[:, (kc * 4 + qt) * 128:
                                               (kc * 4 + qt) * 128 + 128],
                     "y")
        if DBG and mac == 1:
            em.waitm("sync", "cV1", pf + "ye7_3")
            em.dma("sync", dbg_t["dbg_y"][:, :], o6_sb[:, :], ctr="cSt")
        for qt in range(4):
            yt = o6_sb[:, qt * D:(qt + 1) * D]
            ht = h_sb[:, qt * D:(qt + 1) * D]
            em.waitm("scalar", "cV1", pf + f"ye7_{qt}")
            if qt == 0:
                em.waitm("scalar", "cV2", pf + "x23")
            else:
                em.waitm("scalar", "cV2", pf + f"ot{qt - 1}")
            ln_stats(em, yt, pf + f"l4{qt}")
            em.waitm("vector", "cA2", pf + f"l4{qt}rs")
            em.fop("vector", lambda v: v.tensor_scalar(
                va_sb[:, :], m_sb[:, :], rs_sb[:, :], -1.0, ALU.mult,
                ALU.mult), "cV2")
            em.op("vector", lambda v, ht=ht, yt=yt: v.scalar_tensor_tensor(
                ht, yt, rs_sb[:, :], ht, ALU.mult, ALU.add))
            em.op("vector", lambda v, ht=ht: v.tensor_scalar(
                ht, ht, va_sb[:, :], None, ALU.add), "cV2", 1, pf + f"ot{qt}")
            em.waitm("gpsimd", "cV2", pf + f"ot{qt}")
            em.dma("gpsimd",
                   out_c[mac * 512 + qt * 128:mac * 512 + qt * 128 + 128, :],
                   h_sb[:, qt * D:(qt + 1) * D], ctr="cSt",
                   mark=pf + f"os{qt}")
        em.waitm("gpsimd", "cSt", pf + "os3")

    run_phase("p56_0", lambda em: ph56(em, 0))
    nc.all_engine_barrier()
    run_phase("p56_1", lambda em: ph56(em, 1))

    if DBG:
        def ph_dbg_b(em):
            em.dma("sync", dbg_t["dbg_xnt"][:, :], xnT_g[0:D, :], ctr="cSt")
            em.dma("sync", dbg_t["dbg_atslab"][:, :], at_slab[:, :], ctr="cSt")
            em.dma("sync", dbg_t["dbg_atloc"][:, :], at_loc[:, :], ctr="cSt")
            em.dma("sync", dbg_t["dbg_wf"][:, :], comb_g[512:1024, :],
                   ctr="cSt")
            em.dma("sync", dbg_t["dbg_slab"][:, :], comb_slab[512:1024, :],
                   ctr="cSt")
            em.dma("sync", dbg_t["dbg_wo"][:, :], comb_g[0:512, :],
                   ctr="cSt", mark="dbgb")
            em.waitm("sync", "cSt", "dbgb")
        nc.all_engine_barrier()
        run_phase("dbg_b", ph_dbg_b)

    p56s.close()
    stack.close()
    return nc


_NC_CACHE = None


def _get_nc():
    global _NC_CACHE
    if _NC_CACHE is None:
        _NC_CACHE = build_nc()
    return _NC_CACHE


def kernel(**inputs):
    x = np.ascontiguousarray(
        np.asarray(inputs["L0_attn_hidden_states"]).reshape(S, D),
        dtype=np.float32)
    Wq = np.asarray(inputs["L0_attn_Wq"], dtype=np.float32)
    Wk = np.asarray(inputs["L0_attn_Wk"], dtype=np.float32)
    Wv = np.asarray(inputs["L0_attn_Wv"], dtype=np.float32)
    Wo = np.asarray(inputs["L0_attn_Wo"], dtype=np.float32)
    pk = np.ascontiguousarray(np.asarray(inputs["L0_ffn_weight_packed"]),
                              dtype=np.int32)
    sc = np.ascontiguousarray(np.asarray(inputs["L0_ffn_scales"]),
                              dtype=np.float16)
    def sclx(s):
        aidx = np.arange(2048)
        blk = 8 * (aidx // 128) + (aidx % 128) // 16
        return np.ascontiguousarray(s.T[blk, :])

    WqT = np.ascontiguousarray(Wq.T)
    WkT = np.ascontiguousarray(Wk.T)
    WvT = np.ascontiguousarray(Wv.T)
    WoT = np.ascontiguousarray(Wo.T)
    inv = (1.0 / 10000.0 ** (np.arange(0, HD, 2, dtype=np.float32) / HD))
    pos = np.arange(S, dtype=np.float32)
    fr = pos[:, None] * inv[None, :]
    emb = np.concatenate([fr, fr], axis=-1)
    cosT = np.ascontiguousarray(np.cos(emb).T.astype(np.float32))
    sinT = np.ascontiguousarray(np.sin(emb).T.astype(np.float32))

    in_maps = []
    for c in range(NC):
        in_maps.append({
            "x_c": np.ascontiguousarray(x[c * TPC:(c + 1) * TPC, :]),
            "wqt_s": np.ascontiguousarray(WqT[:, c * 512:(c + 1) * 512]),
            "wkt_s": np.ascontiguousarray(WkT[:, c * 128:(c + 1) * 128]),
            "wvt_s": np.ascontiguousarray(WvT[:, c * 128:(c + 1) * 128]),
            "wot_s": np.ascontiguousarray(WoT[c * 512:(c + 1) * 512, :]),
            "pkt_s": np.ascontiguousarray(pk[c * 512:(c + 1) * 512, :].T),
            "scx_s": sclx(sc[c * 512:(c + 1) * 512, :]),
            "cost": cosT,
            "sint": sinT,
        })
    nc = _get_nc()
    res = run_bass_kernel_spmd(nc, in_maps, core_ids=list(range(NC)))
    out = np.concatenate([r["out_c"] for r in res.results], axis=0)
    return out.reshape(1, S, D)
